# version 02
# baseline (speedup 1.0000x reference)
import numpy as np
import concourse.bass as bass
import concourse.mybir as mybir
from contextlib import ExitStack

F32 = mybir.dt.float32
BF16 = mybir.dt.bfloat16
I32 = mybir.dt.int32
AF = mybir.ActivationFunctionType
ALU = mybir.AluOpType
AX = mybir.AxisListType

C = 4096
CC = C // 128
T = 2048
TH = 1024
NH = 16
HD = 128
NA = 10
TA = T + 16
I3 = 11008
IC = I3 // 128
NQT = T // 128
PNC = [4, 5, 6, 7, 2, 3, 0, 1]
RS_ADAPT = float(np.sqrt(HD) / np.sqrt(C))
LN_EPS = 1e-5


class Ctx:
    def __init__(self, nc, stack):
        self.nc = nc
        self.stack = stack
        self.sems = {}
        self.cnt = {}

    def sem(self, name):
        if name not in self.sems:
            self.sems[name] = self.stack.enter_context(self.nc.semaphore(name))
            self.cnt[name] = 0
        return self.sems[name]

    def inc(self, name, n=1):
        h = self.sem(name)
        self.cnt[name] += n
        return h

    def wait(self, eng, name, v=None):
        h = self.sem(name)
        if v is None:
            v = self.cnt[name]
        assert v <= self.cnt[name], f"wait {name} {v} > emitted {self.cnt[name]}"
        if v > 0:
            eng.wait_ge(h, v)

    def v(self, name):
        self.sem(name)
        return self.cnt[name]


def mk_build(debug=False, phases=("p1", "p2", "ex", "p34")):
    nc = bass.Bass("TRN2", target_bir_lowering=False, detect_race_conditions=False)
    dt = nc.dram_tensor

    def din(name, shape, dtype):
        return dt(name, shape, dtype, kind="ExternalInput").ap()

    xT = din("xT", [C, T], F32)
    xTh = din("xTh", [C, TH], F32)
    wqkvT = din("wqkvT", [C, 3 * NH * HD], F32)
    wprojT = din("wprojT", [C, C], F32)
    adT = din("adT", [C, NA], F32)
    gating128 = din("gating128", [128, 1], F32)
    gateT = din("gateT", [I3, C], I32)
    upT = din("upT", [I3, C], I32)
    dnT = din("dnT", [C, I3], I32)
    gsT = din("gsT", [I3, 32], F32)
    usT = din("usT", [I3, 32], F32)
    dsT = din("dsT", [C, 86], F32)
    cosqD = din("cosqD", [128, T], BF16)
    sinqD = din("sinqD", [128, T], BF16)
    coskD = din("coskD", [128, T], BF16)
    sinkD = din("sinkD", [128, T], BF16)
    identD = din("identD", [128, 128], BF16)
    swapD = din("swapD", [128, 128], BF16)
    onesD = din("onesD", [128, 128], BF16)
    maskD = din("maskD", [128, 128], F32)
    nonceD = din("nonceD", [1, 128], I32)

    yout = dt("yout", [C, TH], F32, kind="ExternalOutput").ap()
    skind = "ExternalOutput" if debug else "Internal"
    q_dr = dt("q_dr", [NH, 128, T], BF16, kind=skind).ap()
    k_dr = dt("k_dr", [NH, 128, T], BF16, kind=skind).ap()
    v_dr = dt("v_dr", [NH, NQT, 128, 128], BF16, kind=skind).ap()
    yall = dt("yall", [NH, 128, T], BF16, kind=skind).ap()
    ymine = dt("ymine", [NH, 128, TH], BF16, kind=skind).ap()
    ypart = dt("ypart", [NH, 128, TH], BF16, kind=skind).ap()
    x1_dr = dt("x1_dr", [CC, 128, TH], F32, kind=skind).ap()
    sspill = dt("sspill", [IC, 128, TH // 2], BF16, kind=skind).ap()
    wdeq_dr = dt("wdeq_dr", [CC, IC, 128, 128], BF16, kind=skind).ap()
    sh_y = dt("sh_y", [8, NH, 128, TH], BF16, kind="Internal", addr_space="Shared").ap()
    sh_fl = dt("sh_fl", [8, 4, 1, 128], I32, kind="Internal", addr_space="Shared").ap()

    st = ExitStack()
    with st:
        ctx = Ctx(nc, st)
        en = st.enter_context

        id_sb = en(nc.sbuf_tensor("id_sb", [128, 128], BF16))
        sw_sb = en(nc.sbuf_tensor("sw_sb", [128, 128], BF16))
        on_sb = en(nc.sbuf_tensor("on_sb", [128, 128], BF16))
        mk_sb = en(nc.sbuf_tensor("mk_sb", [128, 128], F32))
        gat_sb = en(nc.sbuf_tensor("gat_sb", [128, 1], F32))
        non_sb = en(nc.sbuf_tensor("non_sb", [1, 128], I32))
        fl_sb = en(nc.sbuf_tensor("fl_sb", [1, 128], I32))
        ak_sb = en(nc.sbuf_tensor("ak_sb", [128, NH, 16], BF16))
        av_sb = en(nc.sbuf_tensor("av_sb", [16, NH, 128], BF16))
        zc_sb = en(nc.sbuf_tensor("zc_sb", [128, 1], F32))
        eps_sb = en(nc.sbuf_tensor("eps_sb", [128, 1], F32))
        nc.const_aps.register(F32, 0.0, zc_sb[:])
        nc.const_aps.register(F32, float(LN_EPS), eps_sb[:])
        pb = [en(nc.psum_tensor(f"pb{i}", [128, 512], F32)) for i in range(8)]

        blk = en(nc.Block())

        env = dict(
            id_sb=id_sb, sw_sb=sw_sb, on_sb=on_sb, mk_sb=mk_sb, gat_sb=gat_sb,
            non_sb=non_sb, fl_sb=fl_sb, ak_sb=ak_sb, av_sb=av_sb, pb=pb,
            xT=xT, xTh=xTh, wqkvT=wqkvT, wprojT=wprojT, adT=adT,
            gating128=gating128, gateT=gateT, upT=upT, dnT=dnT,
            gsT=gsT, usT=usT, dsT=dsT, cosqD=cosqD, sinqD=sinqD,
            coskD=coskD, sinkD=sinkD, identD=identD, swapD=swapD,
            onesD=onesD, maskD=maskD, nonceD=nonceD, yout=yout,
            q_dr=q_dr, k_dr=k_dr, v_dr=v_dr, yall=yall, ymine=ymine,
            ypart=ypart, x1_dr=x1_dr, sspill=sspill, wdeq_dr=wdeq_dr,
            sh_y=sh_y, sh_fl=sh_fl,
        )

        @blk.sync
        def _(sync_eng):
            SY, GP, VE, SC, PE = nc.sync, nc.gpsimd, nc.vector, nc.scalar, nc.tensor

            def sdma(out, in_):
                h = ctx.inc("sq", 16)
                SY.dma_start(out=out, in_=in_).then_inc(h, 16)
                return ctx.v("sq")

            def gdma(out, in_):
                h = ctx.inc("gq", 16)
                GP.dma_start(out=out, in_=in_).then_inc(h, 16)
                return ctx.v("gq")

            def barrier(*engs):
                for eng in engs:
                    for s in ("sq", "gq", "pe", "dv", "ac", "gp"):
                        ctx.wait(eng, s)

            sdma(id_sb[:], identD)
            sdma(sw_sb[:], swapD)
            sdma(on_sb[:], onesD)
            sdma(mk_sb[:], maskD)
            sdma(gat_sb[:], gating128)
            sdma(non_sb[:], nonceD)
            GP.memset(zc_sb[:], 0.0)
            GP.memset(eps_sb[:], float(LN_EPS)).then_inc(ctx.inc("gp"), 1)

            env.update(sdma=sdma, gdma=gdma, barrier=barrier)
            if "p1" in phases:
                emit_p1(nc, ctx, env)
            if "p2" in phases:
                emit_p2(nc, ctx, env)
            if "ex" in phases:
                emit_exchange(nc, ctx, env)
            if "p34" in phases:
                emit_p34(nc, ctx, env)

            for s in ("sq", "gq", "pe", "dv", "ac", "gp"):
                ctx.wait(nc.sync, s)

    return nc



def emit_p1(nc, ctx, ns):
    SY, GP, VE, SC, PE = nc.sync, nc.gpsimd, nc.vector, nc.scalar, nc.tensor
    sdma, gdma, barrier = ns["sdma"], ns["gdma"], ns["barrier"]
    pb = ns["pb"]
    on_sb, id_sb, sw_sb = ns["on_sb"], ns["id_sb"], ns["sw_sb"]
    ak_sb, av_sb = ns["ak_sb"], ns["av_sb"]
    xT, wqkvT, adT = ns["xT"], ns["wqkvT"], ns["adT"]
    cosqD, sinqD, coskD, sinkD = ns["cosqD"], ns["sinqD"], ns["coskD"], ns["sinkD"]
    q_dr, k_dr, v_dr = ns["q_dr"], ns["k_dr"], ns["v_dr"]

    with ExitStack() as p1:
        e = p1.enter_context
        xl = e(nc.sbuf_tensor("xl", [128, CC, TA], BF16))
        cq_sb = e(nc.sbuf_tensor("cq_sb", [128, T], BF16))
        sq_sb = e(nc.sbuf_tensor("sq_sb", [128, T], BF16))
        ck_sb = e(nc.sbuf_tensor("ck_sb", [128, T], BF16))
        sk_sb = e(nc.sbuf_tensor("sk_sb", [128, T], BF16))
        xst = [e(nc.sbuf_tensor(f"xst{i}", [128, T], F32)) for i in range(2)]
        sqst = e(nc.sbuf_tensor("sqst", [128, T], BF16))
        a_bf = e(nc.sbuf_tensor("a_bf", [128, T], BF16))
        b_bf = e(nc.sbuf_tensor("b_bf", [128, T], BF16))
        wt = [e(nc.sbuf_tensor(f"wt{i}", [128, CC, 128], BF16)) for i in range(2)]
        qraw = e(nc.sbuf_tensor("qraw", [128, TH], BF16))
        t2 = e(nc.sbuf_tensor("t2", [128, TH], BF16))
        t3 = e(nc.sbuf_tensor("t3", [128, TH], BF16))
        qf = e(nc.sbuf_tensor("qf", [128, TH], BF16))
        vstage = e(nc.sbuf_tensor("vstage", [128, 512], BF16))
        adst = e(nc.sbuf_tensor("adst", [128, 16], BF16))

        sdma(cq_sb[:], cosqD)
        sdma(sq_sb[:], sinqD)
        sdma(ck_sb[:], coskD)
        sdma(sk_sb[:], sinkD)

        gdma(xl[:, :, T:T + NA], adT.rearrange("(cc p) a -> p cc a", p=128))
        ad_gq = ctx.v("gq")
        ctx.wait(GP, "gq", ad_gq)
        GP.memset(xl[:, :, T + NA:TA], 0.0).then_inc(ctx.inc("gp"), 1)
        gp_pad = ctx.v("gp")

        xT_c = xT.rearrange("(cc p) t -> cc p t", p=128)
        copy_dv = {}
        for cc in range(CC):
            buf = xst[cc % 2]
            if cc >= 2:
                ctx.wait(SY, "dv", copy_dv[cc - 2])
            ld = sdma(buf[:], xT_c[cc])
            ctx.wait(VE, "sq", ld)
            VE.tensor_copy(xl[:, cc, 0:T], buf[:]).then_inc(ctx.inc("dv"), 1)
            copy_dv[cc] = ctx.v("dv")
            ctx.wait(SC, "dv", ctx.v("dv"))
            if cc == 0:
                ctx.wait(SC, "gp", ctx.v("gp"))
            if cc >= 1:
                ctx.wait(SC, "pe", ctx.v("pe"))
            SC.activation(sqst[:], xl[:, cc, 0:T], AF.Square).then_inc(ctx.inc("ac"), 1)
            ctx.wait(PE, "dv", ctx.v("dv"))
            ctx.wait(PE, "ac", ctx.v("ac"))
            for n in range(4):
                PE.matmul(pb[n][:], on_sb[:], xl[:, cc, n * 512:(n + 1) * 512],
                          start=(cc == 0), stop=(cc == CC - 1))
            last = None
            for n in range(4):
                last = PE.matmul(pb[4 + n][:], on_sb[:], sqst[:, n * 512:(n + 1) * 512],
                                 start=(cc == 0), stop=(cc == CC - 1))
            last.then_inc(ctx.inc("pe"), 1)
        pe_stats = ctx.v("pe")

        m_f, var_f = xst[0], xst[1]
        ctx.wait(VE, "pe", pe_stats)
        for n in range(4):
            sl = slice(n * 512, (n + 1) * 512)
            VE.tensor_scalar(m_f[:, sl], pb[n][:], 1.0 / C, None, op0=ALU.mult)
        for n in range(4):
            sl = slice(n * 512, (n + 1) * 512)
            VE.tensor_tensor(var_f[:, sl], m_f[:, sl], m_f[:, sl], op=ALU.mult)
        last = None
        for n in range(4):
            sl = slice(n * 512, (n + 1) * 512)
            last = VE.scalar_tensor_tensor(var_f[:, sl], pb[4 + n][:], 1.0 / C, var_f[:, sl],
                                           op0=ALU.mult, op1=ALU.subtract)
        last.then_inc(ctx.inc("dv"), 1)
        dv_var = ctx.v("dv")
        ctx.wait(SC, "dv", dv_var)
        SC.activation(var_f[:], var_f[:], AF.Ln, bias=LN_EPS, scale=1.0)
        SC.activation(a_bf[:], var_f[:], AF.Exp, scale=-0.5).then_inc(ctx.inc("ac"), 1)
        ctx.wait(VE, "ac", ctx.v("ac"))
        VE.scalar_tensor_tensor(b_bf[:], m_f[:], -1.0, a_bf[:],
                                op0=ALU.mult, op1=ALU.mult).then_inc(ctx.inc("dv"), 1)

        for cc in range(CC):
            for half in range(2):
                sl = slice(half * TH, (half + 1) * TH)
                tt = t2 if half == 0 else t3
                VE.tensor_tensor(tt[:], xl[:, cc, sl], a_bf[:, sl], op=ALU.mult)
                last = VE.tensor_tensor(xl[:, cc, sl], tt[:], b_bf[:, sl], op=ALU.add)
        last.then_inc(ctx.inc("dv"), 1)
        dv_ln1 = ctx.v("dv")

        pe_wt_free = [0, 0]
        dv_bank_free = ctx.v("dv")
        gq_qf_free = 0
        gq_vstage_free = 0
        ctx.wait(PE, "gp", gp_pad)
        wq_c = wqkvT
        for oc in range(48):
            par = oc % 2
            wtb = wt[par]
            ctx.wait(GP, "pe", pe_wt_free[par])
            wld = gdma(wtb[:], wqkvT[:, oc * 128:(oc + 1) * 128]
                       .rearrange("(cc p) o -> p cc o", p=128))
            kind = "q" if oc < 16 else ("k" if oc < 32 else "v")
            head = oc % 16
            ctx.wait(PE, "gq", wld)
            ctx.wait(PE, "dv", dv_bank_free)
            for half in range(2):
                for cc in range(CC):
                    for n in range(2):
                        last = PE.matmul(pb[half * 2 + n][:], wtb[:, cc, :],
                                         xl[:, cc, half * TH + n * 512: half * TH + (n + 1) * 512],
                                         start=(cc == 0), stop=(cc == CC - 1))
            if kind in ("k", "v"):
                for cc in range(CC):
                    last = PE.matmul(pb[4][:, 0:16], wtb[:, cc, :], xl[:, cc, T:TA],
                                     start=(cc == 0), stop=(cc == CC - 1))
            last.then_inc(ctx.inc("pe"), 1)
            pe_wt_free[par] = ctx.v("pe")
            pe_qkv = ctx.v("pe")

            if kind in ("q", "k"):
                csb = cq_sb if kind == "q" else ck_sb
                ssb = sq_sb if kind == "q" else sk_sb
                dest = q_dr if kind == "q" else k_dr
                for half in range(2):
                    ctx.wait(VE, "pe", ctx.v("pe"))
                    for n in range(2):
                        last = VE.tensor_copy(qraw[:, n * 512:(n + 1) * 512], pb[half * 2 + n][:])
                    last.then_inc(ctx.inc("dv"), 1)
                    dv_qraw = ctx.v("dv")
                    ctx.wait(PE, "dv", dv_qraw)
                    PE.matmul(pb[5][:], sw_sb[:], qraw[:, 0:512], start=True, stop=True)
                    PE.matmul(pb[6][:], sw_sb[:], qraw[:, 512:1024],
                              start=True, stop=True).then_inc(ctx.inc("pe"), 1)
                    pe_sw = ctx.v("pe")
                    hs = slice(half * TH, (half + 1) * TH)
                    ctx.wait(VE, "gq", gq_qf_free)
                    VE.tensor_tensor(t2[:], qraw[:], csb[:, hs], op=ALU.mult)
                    ctx.wait(VE, "pe", pe_sw)
                    VE.tensor_tensor(t3[:, 0:512], pb[5][:],
                                     ssb[:, half * TH: half * TH + 512], op=ALU.mult)
                    VE.tensor_tensor(t3[:, 512:1024], pb[6][:],
                                     ssb[:, half * TH + 512:(half + 1) * TH], op=ALU.mult)
                    VE.tensor_tensor(qf[:], t2[:], t3[:], op=ALU.add).then_inc(ctx.inc("dv"), 1)
                    ctx.wait(GP, "dv", ctx.v("dv"))
                    gq_qf_free = gdma(dest[head][:, half * TH:(half + 1) * TH], qf[:])
                dv_bank_free = ctx.v("dv")
                if kind == "k":
                    ctx.wait(VE, "pe", pe_qkv)
                    VE.tensor_scalar(ak_sb[:, head, 0:16], pb[4][:, 0:16], RS_ADAPT, None,
                                     op0=ALU.mult).then_inc(ctx.inc("dv"), 1)
                    dv_bank_free = ctx.v("dv")
            else:
                for half in range(2):
                    ctx.wait(VE, "pe", ctx.v("pe"))
                    for n in range(2):
                        last = VE.tensor_copy(qraw[:, n * 512:(n + 1) * 512], pb[half * 2 + n][:])
                    last.then_inc(ctx.inc("dv"), 1)
                    dv_vraw = ctx.v("dv")
                    ctx.wait(PE, "dv", dv_vraw)
                    for ktg in range(2):
                        bank = pb[5 + ktg]
                        lastt = None
                        for j in range(4):
                            kt = ktg * 4 + j
                            lastt = PE.matmul(bank[:, j * 128:(j + 1) * 128],
                                              qraw[:, kt * 128:(kt + 1) * 128], id_sb[:],
                                              start=True, stop=True)
                        lastt.then_inc(ctx.inc("pe"), 1)
                        ctx.wait(VE, "pe", ctx.v("pe"))
                        ctx.wait(VE, "gq", gq_vstage_free)
                        VE.tensor_copy(vstage[:], bank[:]).then_inc(ctx.inc("dv"), 1)
                        ctx.wait(GP, "dv", ctx.v("dv"))
                        g4 = half * 8 + ktg * 4
                        gq_vstage_free = gdma(
                            v_dr[head, g4:g4 + 4].rearrange("j p q -> p j q"),
                            vstage[:].rearrange("p (j q) -> p j q", q=128))
                dv_bank_free = ctx.v("dv")
                ctx.wait(VE, "pe", pe_qkv)
                if oc == 32:
                    VE.memset(adst[:], 0.0)
                VE.tensor_copy(adst[:, 0:10], pb[4][:, 0:10]).then_inc(ctx.inc("dv"), 1)
                ctx.wait(PE, "dv", ctx.v("dv"))
                PE.matmul(pb[7][0:16, 0:128], adst[:], id_sb[:],
                          start=True, stop=True).then_inc(ctx.inc("pe"), 1)
                ctx.wait(VE, "pe", ctx.v("pe"))
                VE.tensor_copy(av_sb[:, head, :], pb[7][0:16, 0:128]).then_inc(ctx.inc("dv"), 1)
                dv_bank_free = ctx.v("dv")

        barrier(SY, GP, VE, SC, PE)



def emit_p2(nc, ctx, ns):
    SY, GP, VE, SC, PE = nc.sync, nc.gpsimd, nc.vector, nc.scalar, nc.tensor
    sdma, gdma, barrier = ns["sdma"], ns["gdma"], ns["barrier"]
    pb = ns["pb"]
    id_sb, mk_sb, gat_sb = ns["id_sb"], ns["mk_sb"], ns["gat_sb"]
    ak_sb, av_sb = ns["ak_sb"], ns["av_sb"]
    q_dr, k_dr, v_dr, yall = ns["q_dr"], ns["k_dr"], ns["v_dr"], ns["yall"]

    with ExitStack() as p2:
        e = p2.enter_context
        qh = [e(nc.sbuf_tensor(f"qh{i}", [128, T], BF16)) for i in range(2)]
        kh = [e(nc.sbuf_tensor(f"kh{i}", [128, T], BF16)) for i in range(2)]
        vh = [e(nc.sbuf_tensor(f"vh{i}", [128, NQT, 128], BF16)) for i in range(2)]
        probs = e(nc.sbuf_tensor("probs", [128, T], BF16))
        probsN = e(nc.sbuf_tensor("probsN", [128, T], BF16))
        pT = e(nc.sbuf_tensor("pT", [128, NQT, 128], BF16))
        nm4 = e(nc.sbuf_tensor("nm4", [128, 4], F32))
        nmc = e(nc.sbuf_tensor("nmc", [128, 1], F32))
        sac4 = e(nc.sbuf_tensor("sac4", [128, 4], F32))
        ssumc = e(nc.sbuf_tensor("ssumc", [128, 1], F32))
        rinv = e(nc.sbuf_tensor("rinv", [128, 1], F32))
        anm = e(nc.sbuf_tensor("anm", [128, 1], F32))
        asac = e(nc.sbuf_tensor("asac", [128, 1], F32))
        arin = e(nc.sbuf_tensor("arin", [128, 1], F32))
        aprobs = e(nc.sbuf_tensor("aprobs", [128, 16], BF16))
        paT = e(nc.sbuf_tensor("paT", [16, 128], BF16))
        ystage = e(nc.sbuf_tensor("ystage", [128, 128], BF16))
        padt = e(nc.sbuf_tensor("padt", [128, 1], F32))

        VE.memset(aprobs[:], 0.0).then_inc(ctx.inc("dv"), 1)

        ld_done = [0, 0]
        pe_done_head = [0, 0]

        def load_head(head):
            par = head % 2
            ctx.wait(SY, "pe", pe_done_head[par])
            sdma(qh[par][:], q_dr[head])
            sdma(kh[par][:], k_dr[head])
            ld_done[par] = sdma(vh[par][:],
                                v_dr[head].rearrange("k p q -> p k q"))

        load_head(0)
        sc_probs_used = 0
        dv_banks_used = 0
        gq_ystage = 0
        pe_y = 0

        for head in range(NH):
            par = head % 2
            if head + 1 < NH:
                load_head(head + 1)
            for qt in range(NQT):
                n = (qt + 1) * 128
                nb = (n + 511) // 512
                qtile = qh[par][:, qt * 128:(qt + 1) * 128]
                ctx.wait(PE, "sq", ld_done[par])
                ctx.wait(PE, "ac", sc_probs_used)
                ctx.wait(PE, "dv", dv_banks_used)
                last = None
                for j in range(nb):
                    nj = min(512, n - j * 512)
                    last = PE.matmul(pb[j][:, 0:nj], qtile, kh[par][:, j * 512:j * 512 + nj],
                                     start=True, stop=True)
                PE.matmul(pb[4][:, 0:16], qtile, ak_sb[:, head, :],
                          start=True, stop=True).then_inc(ctx.inc("pe"), 1)
                pe_sc = ctx.v("pe")
                jl, off = (qt * 128) // 512, (qt * 128) % 512
                ctx.wait(VE, "pe", pe_sc)
                VE.tensor_tensor(pb[jl][:, off:off + 128], pb[jl][:, off:off + 128],
                                 mk_sb[:], op=ALU.add)
                for j in range(nb):
                    nj = min(512, n - j * 512)
                    VE.reduce_max(nm4[:, j:j + 1], pb[j][:, 0:nj], axis=AX.X, negate=True)
                VE.reduce_max(anm[:], pb[4][:, 0:10], axis=AX.X, negate=True)
                VE.tensor_copy(padt[:], nm4[:, 0:1])
                VE.tensor_copy(padt[:], nm4[:, 0:1])
                VE.tensor_reduce(nmc[:], nm4[:, 0:nb], axis=AX.X,
                                 op=ALU.min).then_inc(ctx.inc("dv"), 1)
                dv_max = ctx.v("dv")
                ctx.wait(SC, "dv", dv_max)
                for j in range(nb):
                    nj = min(512, n - j * 512)
                    SC.activation(probs[:, j * 512:j * 512 + nj], pb[j][:, 0:nj], AF.Exp,
                                  bias=nmc[:], accum_out=sac4[:, j:j + 1])
                SC.activation(aprobs[:, 0:10], pb[4][:, 0:10], AF.Exp,
                              bias=anm[:], accum_out=asac[:]).then_inc(ctx.inc("ac"), 1)
                sc_probs_used = ctx.v("ac")
                dv_banks_used = dv_max
                ctx.wait(VE, "ac", sc_probs_used)
                if nb > 1:
                    VE.tensor_reduce(ssumc[:], sac4[:, 0:nb], axis=AX.X, op=ALU.add)
                    VE.tensor_copy(padt[:], nm4[:, 0:1])
                    VE.tensor_copy(padt[:], nm4[:, 0:1])
                    VE.reciprocal(rinv[:], ssumc[:])
                else:
                    VE.reciprocal(rinv[:], sac4[:, 0:1])
                VE.reciprocal(arin[:], asac[:])
                VE.tensor_copy(padt[:], nm4[:, 0:1])
                VE.tensor_copy(padt[:], nm4[:, 0:1])
                VE.tensor_scalar(probsN[:, 0:n], probs[:, 0:n], rinv[:], None, op0=ALU.mult)
                VE.tensor_scalar(aprobs[:, 0:10], aprobs[:, 0:10], arin[:], gat_sb[:],
                                 op0=ALU.mult, op1=ALU.mult).then_inc(ctx.inc("dv"), 1)
                dv_pn = ctx.v("dv")
                ctx.wait(PE, "dv", dv_pn)
                for g in range((qt + 1 + 3) // 4):
                    cnt = min(4, qt + 1 - g * 4)
                    if g >= 1:
                        ctx.wait(PE, "dv", ctx.v("dv"))
                    lastt = None
                    for j in range(cnt):
                        kt = g * 4 + j
                        lastt = PE.matmul(pb[5][:, j * 128:(j + 1) * 128],
                                          probsN[:, kt * 128:(kt + 1) * 128], id_sb[:],
                                          start=True, stop=True)
                    lastt.then_inc(ctx.inc("pe"), 1)
                    ctx.wait(VE, "pe", ctx.v("pe"))
                    VE.tensor_copy(pT[:, g * 4:g * 4 + cnt, :],
                                   pb[5][:, 0:cnt * 128].rearrange("p (j q) -> p j q", q=128)
                                   ).then_inc(ctx.inc("dv"), 1)
                PE.matmul(pb[7][0:16, 0:128], aprobs[:], id_sb[:],
                          start=True, stop=True).then_inc(ctx.inc("pe"), 1)
                ctx.wait(VE, "pe", ctx.v("pe"))
                VE.tensor_copy(paT[:], pb[7][0:16, 0:128]).then_inc(ctx.inc("dv"), 1)
                dv_pT = ctx.v("dv")
                ctx.wait(PE, "dv", dv_pT)
                for kt in range(qt + 1):
                    PE.matmul(pb[6][:, 0:128], vh[par][:, kt, :], pT[:, kt, :],
                              start=(kt == 0), stop=False)
                PE.matmul(pb[6][:, 0:128], av_sb[:, head, :], paT[:],
                          start=False, stop=True).then_inc(ctx.inc("pe"), 1)
                pe_y = ctx.v("pe")
                ctx.wait(VE, "pe", pe_y)
                ctx.wait(VE, "gq", gq_ystage)
                VE.tensor_copy(ystage[:], pb[6][:, 0:128]).then_inc(ctx.inc("dv"), 1)
                ctx.wait(GP, "dv", ctx.v("dv"))
                gq_ystage = gdma(yall[head][:, qt * 128:(qt + 1) * 128], ystage[:])
            pe_done_head[par] = ctx.v("pe")

        barrier(SY, GP, VE, SC, PE)




def emit_exchange(nc, ctx, ns):
    SY, GP, VE, SC, PE = nc.sync, nc.gpsimd, nc.vector, nc.scalar, nc.tensor
    yall, ymine, ypart = ns["yall"], ns["ymine"], ns["ypart"]
    sh_y, sh_fl = ns["sh_y"], ns["sh_fl"]
    non_sb, fl_sb = ns["non_sb"], ns["fl_sb"]

    import copy
    save = dict(ctx.cnt)
    post = None
    for arm in nc.switch_core_id(engines=[nc.gpsimd], n=128):
        ctx.cnt = dict(save)
        a = int(arm)
        slot = PNC.index(a % 8)
        h = slot % 2
        GP.dma_start(out=ymine, in_=yall[:, :, h * TH:(h + 1) * TH]).then_inc(ctx.inc("gq", 16), 16)
        GP.dma_start(out=sh_y[slot], in_=yall[:, :, (1 - h) * TH:(2 - h) * TH]).then_inc(ctx.inc("gq", 16), 16)
        ctx.wait(GP, "gq")
        GP.dma_start(out=sh_fl[slot, 0], in_=non_sb[:]).then_inc(ctx.inc("gq", 16), 16)
        ctx.wait(GP, "gq")
        with (
            nc.gpsimd.register(f"xc{a}") as cont,
            nc.gpsimd.register(f"xf{a}") as fv,
            nc.gpsimd.register(f"xn{a}") as nv,
            nc.gpsimd.register(f"xe{a}") as exp,
        ):
            GP.reg_load(nv, non_sb[0:1, 0:1])
            GP.reg_mov(cont, 1)
            GP.reg_mov(exp, 16)
            with nc.gpsimd.While(cont):
                GP.dma_start(out=fl_sb[:], in_=sh_fl[slot ^ 1, 0]).then_inc(ctx.sem("px"), 16)
                GP.wait_ge(ctx.sem("px"), exp)
                GP.reg_alu(exp, exp, 16, ALU.add)
                GP.reg_load(fv, fl_sb[0:1, 0:1])
                GP.reg_alu(cont, fv, nv, ALU.not_equal)
        GP.dma_start(out=ypart, in_=sh_y[slot ^ 1]).then_inc(ctx.inc("gq", 16), 16)
        ctx.wait(GP, "gq")
        post = dict(ctx.cnt)
    ctx.cnt = post
    ns["barrier"](SY, GP, VE, SC, PE)




def emit_p34(nc, ctx, ns):
    SY, GP, VE, SC, PE = nc.sync, nc.gpsimd, nc.vector, nc.scalar, nc.tensor
    sdma, gdma, barrier = ns["sdma"], ns["gdma"], ns["barrier"]
    pb = ns["pb"]
    id_sb, on_sb = ns["id_sb"], ns["on_sb"]
    ymine, ypart, x1_dr = ns["ymine"], ns["ypart"], ns["x1_dr"]
    wprojT, xTh, yout = ns["wprojT"], ns["xTh"], ns["yout"]
    gateT, upT, dnT = ns["gateT"], ns["upT"], ns["dnT"]
    gsT, usT, dsT = ns["gsT"], ns["usT"], ns["dsT"]
    sspill, wdeq_dr = ns["sspill"], ns["wdeq_dr"]

    with ExitStack() as p34:
        e = p34.enter_context
        x2buf = e(nc.sbuf_tensor("x2buf", [128, CC, TH], BF16))

        with ExitStack() as p3:
            e3 = p3.enter_context
            ybuf = e3(nc.sbuf_tensor("ybuf", [128, CC, TH], BF16))
            wt = [e3(nc.sbuf_tensor(f"pwt{i}", [128, CC, 128], BF16)) for i in range(2)]
            xst = [e3(nc.sbuf_tensor(f"pxst{i}", [128, TH], F32)) for i in range(2)]
            x1f = e3(nc.sbuf_tensor("x1f", [128, TH], F32))
            sqb = e3(nc.sbuf_tensor("sqb", [128, TH], BF16))
            a2 = e3(nc.sbuf_tensor("a2", [128, TH], BF16))
            b2 = e3(nc.sbuf_tensor("b2", [128, TH], BF16))
            t2b = e3(nc.sbuf_tensor("t2b", [128, TH], BF16))

            gdma(ybuf[:, 0:NH, :], ymine.rearrange("j p t -> p j t"))
            yld = gdma(ybuf[:, NH:CC, :], ypart.rearrange("j p t -> p j t"))

            pe_wt_free = [0, 0]
            dv_banks_free = ctx.v("dv")
            gq_x1f_free = 0
            for oc in range(CC):
                par = oc % 2
                wtb = wt[par]
                ctx.wait(GP, "pe", pe_wt_free[par])
                wld = gdma(wtb[:], wprojT[:, oc * 128:(oc + 1) * 128]
                           .rearrange("(cc p) o -> p cc o", p=128))
                ctx.wait(SY, "dv", dv_banks_free)
                xld = sdma(xst[par][:], xTh.rearrange("(cc p) t -> cc p t", p=128)[oc])
                ctx.wait(PE, "gq", max(wld, yld))
                ctx.wait(PE, "dv", dv_banks_free)
                bs = 2 * par
                last = None
                for cc in range(CC):
                    for n in range(2):
                        last = PE.matmul(pb[bs + n][:], wtb[:, cc, :],
                                         ybuf[:, cc, n * 512:(n + 1) * 512],
                                         start=(cc == 0), stop=(cc == CC - 1))
                last.then_inc(ctx.inc("pe"), 1)
                pe_wt_free[par] = ctx.v("pe")
                ctx.wait(VE, "pe", ctx.v("pe"))
                ctx.wait(VE, "sq", xld)
                ctx.wait(VE, "gq", gq_x1f_free)
                for n in range(2):
                    VE.tensor_tensor(x1f[:, n * 512:(n + 1) * 512], pb[bs + n][:],
                                     xst[par][:, n * 512:(n + 1) * 512], op=ALU.add)
                VE.tensor_copy(x2buf[:, oc, :], x1f[:]).then_inc(ctx.inc("dv"), 1)
                dv_banks_free = ctx.v("dv")
                ctx.wait(GP, "dv", ctx.v("dv"))
                gq_x1f_free = gdma(x1_dr[oc], x1f[:])

            pe_sq_free = 0
            for cc in range(CC):
                ctx.wait(SC, "dv", dv_banks_free if cc == 0 else ctx.v("dv"))
                ctx.wait(SC, "pe", pe_sq_free)
                SC.activation(sqb[:], x2buf[:, cc, :], AF.Square).then_inc(ctx.inc("ac"), 1)
                ctx.wait(PE, "ac", ctx.v("ac"))
                if cc == 0:
                    ctx.wait(PE, "dv", dv_banks_free)
                last = None
                for n in range(2):
                    PE.matmul(pb[4 + n][:], on_sb[:], x2buf[:, cc, n * 512:(n + 1) * 512],
                              start=(cc == 0), stop=(cc == CC - 1))
                    last = PE.matmul(pb[6 + n][:], on_sb[:], sqb[:, n * 512:(n + 1) * 512],
                                     start=(cc == 0), stop=(cc == CC - 1))
                last.then_inc(ctx.inc("pe"), 1)
                pe_sq_free = ctx.v("pe")
            pe_stats2 = ctx.v("pe")
            m2f, v2f = xst[0], xst[1]
            ctx.wait(VE, "pe", pe_stats2)
            for n in range(2):
                sl = slice(n * 512, (n + 1) * 512)
                VE.tensor_scalar(m2f[:, sl], pb[4 + n][:], 1.0 / C, None, op0=ALU.mult)
            for n in range(2):
                sl = slice(n * 512, (n + 1) * 512)
                VE.tensor_tensor(v2f[:, sl], m2f[:, sl], m2f[:, sl], op=ALU.mult)
            last = None
            for n in range(2):
                sl = slice(n * 512, (n + 1) * 512)
                last = VE.scalar_tensor_tensor(v2f[:, sl], pb[6 + n][:], 1.0 / C, v2f[:, sl],
                                               op0=ALU.mult, op1=ALU.subtract)
            last.then_inc(ctx.inc("dv"), 1)
            ctx.wait(SC, "dv", ctx.v("dv"))
            SC.activation(v2f[:], v2f[:], AF.Ln, bias=LN_EPS, scale=1.0)
            SC.activation(a2[:], v2f[:], AF.Exp, scale=-0.5).then_inc(ctx.inc("ac"), 1)
            ctx.wait(VE, "ac", ctx.v("ac"))
            VE.scalar_tensor_tensor(b2[:], m2f[:], -1.0, a2[:],
                                    op0=ALU.mult, op1=ALU.mult).then_inc(ctx.inc("dv"), 1)
            for cc in range(CC):
                VE.tensor_tensor(t2b[:], x2buf[:, cc, :], a2[:], op=ALU.mult)
                last = VE.tensor_tensor(x2buf[:, cc, :], t2b[:], b2[:], op=ALU.add)
            last.then_inc(ctx.inc("dv"), 1)
            barrier(SY, GP, VE, SC, PE)

        s_st = p34.enter_context(ExitStack())
        s_sbuf = s_st.enter_context(nc.sbuf_tensor("s_sbuf", [128, IC, 512], BF16))

        with ExitStack() as pa:
            ea = pa.enter_context
            gsl = [[ea(nc.sbuf_tensor(f"gsl{m}{i}", [128, 32], F32)) for i in range(2)]
                   for m in range(2)]
            wg_sb = ea(nc.sbuf_tensor("wg_sb", [128, CC, 128], BF16))
            wu_sb = ea(nc.sbuf_tensor("wu_sb", [128, CC, 128], BF16))
            qstage = ea(nc.sbuf_tensor("qstage", [128, T], I32))
            wdq = ea(nc.sbuf_tensor("wdq", [128, T], BF16))
            tbst = ea(nc.sbuf_tensor("tbst", [128, 512], BF16))
            tbst2 = ea(nc.sbuf_tensor("tbst2", [128, 512], BF16))
            s2st = ea(nc.sbuf_tensor("s2st", [128, 512], BF16))


            dv_wdq_free = ctx.v("dv")
            pe_qs_free = 0
            sc_gu_used = ctx.v("ac")
            dv_gu_used = ctx.v("dv")
            gq_s2_free = 0
            pe_w_used = [0, 0]
            for ic in range(IC):
                for mi, (wqd, sclT, wsb) in enumerate(
                        [(gateT, gsT, wg_sb), (upT, usT, wu_sb)]):
                    ctx.wait(SY, "dv", dv_wdq_free)
                    scl = gsl[mi][ic % 2]
                    scl_ld = sdma(scl[:], sclT[ic * 128:(ic + 1) * 128, :])
                    for hh in range(2):
                        ctx.wait(SY, "dv", dv_wdq_free)
                        qld = sdma(qstage[:], wqd[ic * 128:(ic + 1) * 128,
                                                  hh * 2048:(hh + 1) * 2048])
                        ctx.wait(VE, "sq", max(qld, scl_ld))
                        ctx.wait(VE, "pe", pe_qs_free)
                        sv = scl[:, hh * 16:(hh + 1) * 16].unsqueeze(-1) \
                            .broadcast_to([128, 16, 128])
                        VE.scalar_tensor_tensor(
                            wdq[:].rearrange("p (g r) -> p g r", r=128),
                            qstage[:].rearrange("p (g r) -> p g r", r=128),
                            8.0, sv, op0=ALU.subtract, op1=ALU.mult,
                        ).then_inc(ctx.inc("dv"), 1)
                        dv_wdq_free = ctx.v("dv")
                        ctx.wait(PE, "dv", dv_wdq_free)
                        ctx.wait(PE, "pe", pe_w_used[mi])
                        for g in range(4):
                            bank = pb[4 + g % 2]
                            if ic + mi + hh + g > 0:
                                ctx.wait(PE, "dv", ctx.v("dv"))
                            lastt = None
                            for j in range(4):
                                hc2 = g * 4 + j
                                lastt = PE.matmul(bank[:, j * 128:(j + 1) * 128],
                                                  wdq[:, hc2 * 128:(hc2 + 1) * 128], id_sb[:],
                                                  start=True, stop=True)
                            lastt.then_inc(ctx.inc("pe"), 1)
                            ctx.wait(VE, "pe", ctx.v("pe"))
                            VE.tensor_copy(wsb[:, hh * 16 + g * 4: hh * 16 + g * 4 + 4, :],
                                           bank[:].rearrange("p (j q) -> p j q", q=128)
                                           ).then_inc(ctx.inc("dv"), 1)
                        pe_qs_free = ctx.v("pe")
                ctx.wait(PE, "dv", ctx.v("dv"))
                ctx.wait(PE, "ac", sc_gu_used)
                ctx.wait(PE, "dv", dv_gu_used)
                last = None
                for hc in range(CC):
                    for n in range(2):
                        PE.matmul(pb[n][:], wg_sb[:, hc, :], x2buf[:, hc, n * 512:(n + 1) * 512],
                                  start=(hc == 0), stop=(hc == CC - 1))
                        last = PE.matmul(pb[2 + n][:], wu_sb[:, hc, :],
                                         x2buf[:, hc, n * 512:(n + 1) * 512],
                                         start=(hc == 0), stop=(hc == CC - 1))
                last.then_inc(ctx.inc("pe"), 1)
                pe_w_used[0] = pe_w_used[1] = pe_gu = ctx.v("pe")
                ctx.wait(SC, "pe", pe_gu)
                ctx.wait(SC, "dv", dv_gu_used)
                SC.activation(tbst[:], pb[0][:], AF.Silu).then_inc(ctx.inc("ac"), 1)
                ctx.wait(VE, "ac", ctx.v("ac"))
                VE.tensor_tensor(s_sbuf[:, ic, :], tbst[:], pb[2][:], op=ALU.mult) \
                    .then_inc(ctx.inc("dv"), 1)
                SC.activation(tbst2[:], pb[1][:], AF.Silu).then_inc(ctx.inc("ac"), 1)
                sc_gu_used = ctx.v("ac")
                ctx.wait(VE, "ac", ctx.v("ac"))
                ctx.wait(VE, "gq", gq_s2_free)
                VE.tensor_tensor(s2st[:], tbst2[:], pb[3][:], op=ALU.mult) \
                    .then_inc(ctx.inc("dv"), 1)
                dv_gu_used = ctx.v("dv")
                ctx.wait(GP, "dv", ctx.v("dv"))
                gq_s2_free = gdma(sspill[ic], s2st[:])
            barrier(SY, GP, VE, SC, PE)

        with ExitStack() as pbx:
            eb = pbx.enter_context
            dsl = [eb(nc.sbuf_tensor(f"dsl{i}", [128, 86], F32)) for i in range(2)]
            dstage = eb(nc.sbuf_tensor("dstage", [128, 5504], I32))
            ddq = eb(nc.sbuf_tensor("ddq", [128, 5504], BF16))
            wdt = eb(nc.sbuf_tensor("wdt", [128, 43, 128], BF16))
            x1st = eb(nc.sbuf_tensor("x1st", [128, 512], F32))
            outst = eb(nc.sbuf_tensor("outst", [128, 512], F32))


            for thalf in range(2):
                if thalf == 1:
                    barrier(SY, GP, VE, SC, PE)
                    sld = sdma(s_sbuf[:], sspill.rearrange("i p n -> p i n"))
                for hc in range(CC):
                    pbD = pb[6 + hc % 2]
                    if thalf == 0:
                        ctx.wait(SY, "dv", ctx.v("dv"))
                        dsld = sdma(dsl[hc % 2][:], dsT[hc * 128:(hc + 1) * 128, :])
                    for ih in range(2):
                        nic = 43
                        if thalf == 0:
                            ctx.wait(SY, "dv", ctx.v("dv"))
                            dld = sdma(dstage[:], dnT[hc * 128:(hc + 1) * 128,
                                                      ih * 5504:(ih + 1) * 5504])
                            ctx.wait(VE, "sq", max(dld, dsld))
                            ctx.wait(VE, "pe", ctx.v("pe"))
                            dv_ = dsl[hc % 2][:, ih * 43:(ih + 1) * 43].unsqueeze(-1) \
                                .broadcast_to([128, 43, 128])
                            VE.scalar_tensor_tensor(
                                ddq[:].rearrange("p (g r) -> p g r", r=128),
                                dstage[:].rearrange("p (g r) -> p g r", r=128),
                                8.0, dv_, op0=ALU.subtract, op1=ALU.mult,
                            ).then_inc(ctx.inc("dv"), 1)
                            dv_ddq = ctx.v("dv")
                            ctx.wait(PE, "dv", dv_ddq)
                            done = 0
                            for g in range(11):
                                cnt = min(4, nic - g * 4)
                                bank = pb[4 + g % 2]
                                ctx.wait(PE, "dv", ctx.v("dv"))
                                lastt = None
                                for j in range(cnt):
                                    i43 = g * 4 + j
                                    lastt = PE.matmul(bank[:, j * 128:(j + 1) * 128],
                                                      ddq[:, i43 * 128:(i43 + 1) * 128],
                                                      id_sb[:], start=True, stop=True)
                                lastt.then_inc(ctx.inc("pe"), 1)
                                ctx.wait(VE, "pe", ctx.v("pe"))
                                VE.tensor_copy(wdt[:, g * 4:g * 4 + cnt, :],
                                               bank[:, 0:cnt * 128]
                                               .rearrange("p (j q) -> p j q", q=128)
                                               ).then_inc(ctx.inc("dv"), 1)
                                done += cnt
                            dv_wdt = ctx.v("dv")
                            ctx.wait(GP, "dv", dv_wdt)
                            gdma(wdeq_dr[hc, ih * 43:(ih + 1) * 43]
                                 .rearrange("i p q -> p i q"), wdt[:])
                        else:
                            ctx.wait(GP, "pe", ctx.v("pe"))
                            wld = gdma(wdt[:], wdeq_dr[hc, ih * 43:(ih + 1) * 43]
                                       .rearrange("i p q -> p i q"))
                            ctx.wait(PE, "gq", wld)
                            if hc == 0 and ih == 0:
                                ctx.wait(PE, "sq", sld)
                        ctx.wait(PE, "dv", ctx.v("dv"))
                        if ih == 0:
                            ctx.wait(PE, "ac", ctx.v("ac"))
                        last = None
                        for i43 in range(43):
                            last = PE.matmul(pbD[:], wdt[:, i43, :],
                                             s_sbuf[:, ih * 43 + i43, :],
                                             start=(ih == 0 and i43 == 0),
                                             stop=(ih == 1 and i43 == 42))
                        last.then_inc(ctx.inc("pe"), 1)
                    pe_dn = ctx.v("pe")
                    xld = sdma(x1st[:], x1_dr[hc][:, thalf * 512:(thalf + 1) * 512])
                    ctx.wait(VE, "pe", pe_dn)
                    ctx.wait(VE, "sq", xld)
                    ctx.wait(VE, "so", ctx.v("so"))
                    VE.tensor_tensor(outst[:], pbD[:], x1st[:], op=ALU.add) \
                        .then_inc(ctx.inc("dv"), 1)
                    ctx.wait(SY, "dv", ctx.v("dv"))
                    h2 = ctx.inc("so", 16)
                    SY.dma_start(out=yout[hc * 128:(hc + 1) * 128,
                                          thalf * 512:(thalf + 1) * 512],
                                 in_=outst[:]).then_inc(h2, 16)
            barrier(SY, GP, VE, SC, PE)




def rope_tables():
    theta = 1.0 / (10000.0 ** (np.arange(0, HD, 2, dtype=np.float64) / HD))
    idx = np.outer(np.arange(T, dtype=np.float64), theta)
    cos = np.cos(idx)
    sin = np.sin(idx)
    Cf = np.concatenate([cos, cos], axis=1).T
    Sf = np.concatenate([-sin, sin], axis=1).T
    rs = 1.0 / float(np.sqrt(HD))
    return Cf, Sf, rs


def prep_core_inputs(inp, core, nonce):
    import ml_dtypes
    bf = ml_dtypes.bfloat16
    b, h = core // 2, core % 2
    x = np.asarray(inp["L0_attn_x"])
    w_attn = np.asarray(inp["L0_attn_w_attn"])
    w_proj = np.asarray(inp["L0_attn_w_proj"])
    ad = np.asarray(inp["L0_attn_adapter_wte"])
    gating = np.asarray(inp["L0_attn_gating"])
    gq = np.asarray(inp["L0_ffn_gate_q"])
    gs = np.asarray(inp["L0_ffn_gate_s"])
    uq = np.asarray(inp["L0_ffn_up_q"])
    us = np.asarray(inp["L0_ffn_up_s"])
    dq = np.asarray(inp["L0_ffn_down_q"])
    ds = np.asarray(inp["L0_ffn_down_s"])

    xT_ = np.ascontiguousarray(x[b].T.astype(np.float32))
    xTh_ = np.ascontiguousarray(xT_[:, h * TH:(h + 1) * TH])
    rows = np.r_[h * 2048:(h + 1) * 2048,
                 4096 + h * 2048: 4096 + (h + 1) * 2048,
                 8192 + h * 2048: 8192 + (h + 1) * 2048]
    wqkvT_ = np.ascontiguousarray(w_attn[rows].T)
    perm = np.r_[h * 2048:(h + 1) * 2048, (1 - h) * 2048:(2 - h) * 2048]
    wprojT_ = np.ascontiguousarray(w_proj.T[perm])
    Cf, Sf, rs = rope_tables()
    mask = np.where(np.arange(128)[None, :] <= np.arange(128)[:, None], 0.0, -1e9)

    return dict(
        xT=xT_, xTh=xTh_, wqkvT=wqkvT_, wprojT=wprojT_,
        adT=np.ascontiguousarray(ad.T),
        gating128=np.full((128, 1), np.float32(gating[0])),
        gateT=np.ascontiguousarray(gq.T), upT=np.ascontiguousarray(uq.T),
        dnT=np.ascontiguousarray(dq.T),
        gsT=np.ascontiguousarray(gs.T), usT=np.ascontiguousarray(us.T),
        dsT=np.ascontiguousarray(ds.T),
        cosqD=(Cf * rs).astype(bf), sinqD=(Sf * rs).astype(bf),
        coskD=Cf.astype(bf), sinkD=Sf.astype(bf),
        identD=np.eye(128).astype(bf),
        swapD=np.eye(128)[(np.arange(128) + 64) % 128].astype(bf),
        onesD=np.ones((128, 128)).astype(bf),
        maskD=mask.astype(np.float32),
        nonceD=nonce,
    )


from concourse.bass_utils import run_bass_kernel_spmd

_NC_CACHE = [None]


def kernel(**inputs):
    if _NC_CACHE[0] is None:
        _NC_CACHE[0] = mk_build(debug=False)
    nc = _NC_CACHE[0]
    nonce = np.random.randint(1, 2 ** 30, size=(1, 128)).astype(np.int32)
    in_maps = [prep_core_inputs(inputs, c, nonce) for c in range(8)]
    res = run_bass_kernel_spmd(nc, in_maps, core_ids=list(range(8)))
    out = np.empty((4, 2048, 4096), np.float32)
    for c in range(8):
        b, h = c // 2, c % 2
        out[b, h * TH:(h + 1) * TH, :] = np.asarray(res.results[c]["yout"]).T
    return out


# version 05
# speedup vs baseline: 1.2107x; 1.2107x over previous
import numpy as np
import concourse.bass as bass
import concourse.mybir as mybir
from contextlib import ExitStack

F32 = mybir.dt.float32
BF16 = mybir.dt.bfloat16
I32 = mybir.dt.int32
AF = mybir.ActivationFunctionType
ALU = mybir.AluOpType
AX = mybir.AxisListType

C = 4096
CC = C // 128
T = 2048
TH = 1024
NH = 16
HD = 128
NA = 10
TA = T + 16
I3 = 11008
IC = I3 // 128
NQT = T // 128
PNC = [4, 5, 6, 7, 2, 3, 0, 1]
RS_ADAPT = float(np.sqrt(HD) / np.sqrt(C))
LN_EPS = 1e-5


class Ctx:
    def __init__(self, nc, stack):
        self.nc = nc
        self.stack = stack
        self.sems = {}
        self.cnt = {}

    def sem(self, name):
        if name not in self.sems:
            self.sems[name] = self.stack.enter_context(self.nc.semaphore(name))
            self.cnt[name] = 0
        return self.sems[name]

    def inc(self, name, n=1):
        h = self.sem(name)
        self.cnt[name] += n
        return h

    def wait(self, eng, name, v=None):
        h = self.sem(name)
        if v is None:
            v = self.cnt[name]
        assert v <= self.cnt[name], f"wait {name} {v} > emitted {self.cnt[name]}"
        if v > 0:
            eng.wait_ge(h, v)

    def v(self, name):
        self.sem(name)
        return self.cnt[name]


def mk_build(debug=False, phases=("p1", "p2", "ex", "p34")):
    nc = bass.Bass("TRN2", target_bir_lowering=False, detect_race_conditions=False)
    dt = nc.dram_tensor

    def din(name, shape, dtype):
        return dt(name, shape, dtype, kind="ExternalInput").ap()

    xT = din("xT", [C, T], F32)
    xTh = din("xTh", [C, TH], F32)
    wqkvT = din("wqkvT", [C, 3 * NH * HD], F32)
    wprojT = din("wprojT", [C, C], F32)
    adT = din("adT", [C, NA], F32)
    gating128 = din("gating128", [128, 1], F32)
    gateT = din("gateT", [I3, C], I32)
    upT = din("upT", [I3, C], I32)
    dnT = din("dnT", [C, I3], I32)
    gsT = din("gsT", [I3, 32], F32)
    usT = din("usT", [I3, 32], F32)
    dsT = din("dsT", [C, 86], F32)
    cosqD = din("cosqD", [128, T], BF16)
    sinqD = din("sinqD", [128, T], BF16)
    coskD = din("coskD", [128, T], BF16)
    sinkD = din("sinkD", [128, T], BF16)
    identD = din("identD", [128, 128], BF16)
    swapD = din("swapD", [128, 128], BF16)
    onesD = din("onesD", [128, 128], BF16)
    maskD = din("maskD", [128, 128], F32)
    nonceD = din("nonceD", [1, 128], I32)

    yout = dt("yout", [C, TH], F32, kind="ExternalOutput").ap()
    skind = "ExternalOutput" if debug else "Internal"
    q_dr = dt("q_dr", [NH, 128, T], BF16, kind=skind).ap()
    k_dr = dt("k_dr", [NH, 128, T], BF16, kind=skind).ap()
    v_dr = dt("v_dr", [NH, NQT, 128, 128], BF16, kind=skind).ap()
    yall = dt("yall", [NH, 128, T], BF16, kind=skind).ap()
    ymine = dt("ymine", [NH, 128, TH], BF16, kind=skind).ap()
    ypart = dt("ypart", [NH, 128, TH], BF16, kind=skind).ap()
    x1_dr = dt("x1_dr", [CC, 128, TH], F32, kind=skind).ap()
    sspill = dt("sspill", [IC, 128, TH // 2], BF16, kind=skind).ap()
    wdeq_dr = dt("wdeq_dr", [CC, IC, 128, 128], BF16, kind=skind).ap()
    sh_y = dt("sh_y", [8, NH, 128, TH], BF16, kind="Internal", addr_space="Shared").ap()
    sh_fl = dt("sh_fl", [8, 4, 1, 128], I32, kind="Internal", addr_space="Shared").ap()
    dbg_xl = dt("dbg_xl", [128, CC, TA], BF16, kind="ExternalOutput").ap() if debug else None

    st = ExitStack()
    with st:
        ctx = Ctx(nc, st)
        en = st.enter_context

        id_sb = en(nc.sbuf_tensor("id_sb", [128, 128], BF16))
        sw_sb = en(nc.sbuf_tensor("sw_sb", [128, 128], BF16))
        on_sb = en(nc.sbuf_tensor("on_sb", [128, 128], BF16))
        mk_sb = en(nc.sbuf_tensor("mk_sb", [128, 128], F32))
        gat_sb = en(nc.sbuf_tensor("gat_sb", [128, 1], F32))
        non_sb = en(nc.sbuf_tensor("non_sb", [1, 128], I32))
        fl_sb = en(nc.sbuf_tensor("fl_sb", [1, 128], I32))
        ak_sb = en(nc.sbuf_tensor("ak_sb", [128, NH, 16], BF16))
        av_sb = en(nc.sbuf_tensor("av_sb", [16, NH, 128], BF16))
        zc_sb = en(nc.sbuf_tensor("zc_sb", [128, 1], F32))
        eps_sb = en(nc.sbuf_tensor("eps_sb", [128, 1], F32))
        nc.const_aps.register(F32, 0.0, zc_sb[:])
        nc.const_aps.register(F32, float(LN_EPS), eps_sb[:])
        pb = [en(nc.psum_tensor(f"pb{i}", [128, 512], F32)) for i in range(8)]

        blk = en(nc.Block())

        env = dict(
            id_sb=id_sb, sw_sb=sw_sb, on_sb=on_sb, mk_sb=mk_sb, gat_sb=gat_sb,
            non_sb=non_sb, fl_sb=fl_sb, ak_sb=ak_sb, av_sb=av_sb, pb=pb,
            xT=xT, xTh=xTh, wqkvT=wqkvT, wprojT=wprojT, adT=adT,
            gating128=gating128, gateT=gateT, upT=upT, dnT=dnT,
            gsT=gsT, usT=usT, dsT=dsT, cosqD=cosqD, sinqD=sinqD,
            coskD=coskD, sinkD=sinkD, identD=identD, swapD=swapD,
            onesD=onesD, maskD=maskD, nonceD=nonceD, yout=yout,
            q_dr=q_dr, k_dr=k_dr, v_dr=v_dr, yall=yall, ymine=ymine,
            ypart=ypart, x1_dr=x1_dr, sspill=sspill, wdeq_dr=wdeq_dr,
            sh_y=sh_y, sh_fl=sh_fl,
        )

        @blk.sync
        def _(sync_eng):
            SY, GP, VE, SC, PE = nc.sync, nc.gpsimd, nc.vector, nc.scalar, nc.tensor

            def sdma(out, in_):
                h = ctx.inc("sq", 16)
                SY.dma_start(out=out, in_=in_).then_inc(h, 16)
                return ctx.v("sq")

            def gdma(out, in_):
                h = ctx.inc("gq", 16)
                GP.dma_start(out=out, in_=in_).then_inc(h, 16)
                return ctx.v("gq")

            def barrier(*engs):
                for eng in engs:
                    for s in ("sq", "gq", "pe", "dv", "ac", "gp"):
                        ctx.wait(eng, s)

            sdma(id_sb[:], identD)
            sdma(sw_sb[:], swapD)
            sdma(on_sb[:], onesD)
            sdma(mk_sb[:], maskD)
            sdma(gat_sb[:], gating128)
            sdma(non_sb[:], nonceD)
            GP.memset(zc_sb[:], 0.0)
            GP.memset(eps_sb[:], float(LN_EPS)).then_inc(ctx.inc("gp"), 1)

            env.update(sdma=sdma, gdma=gdma, barrier=barrier)
            env["dbg_xl"] = dbg_xl
            if "p1" in phases:
                emit_p1(nc, ctx, env)
            if "p2" in phases:
                emit_p2(nc, ctx, env)
            if "ex" in phases:
                emit_exchange(nc, ctx, env)
            if "p34" in phases:
                emit_p34(nc, ctx, env)

            for s in ("sq", "gq", "pe", "dv", "ac", "gp"):
                ctx.wait(nc.sync, s)

    return nc



def emit_p1(nc, ctx, ns):
    SY, GP, VE, SC, PE = nc.sync, nc.gpsimd, nc.vector, nc.scalar, nc.tensor
    sdma, gdma, barrier = ns["sdma"], ns["gdma"], ns["barrier"]
    pb = ns["pb"]
    on_sb, id_sb, sw_sb = ns["on_sb"], ns["id_sb"], ns["sw_sb"]
    ak_sb, av_sb = ns["ak_sb"], ns["av_sb"]
    xT, wqkvT, adT = ns["xT"], ns["wqkvT"], ns["adT"]
    cosqD, sinqD, coskD, sinkD = ns["cosqD"], ns["sinqD"], ns["coskD"], ns["sinkD"]
    q_dr, k_dr, v_dr = ns["q_dr"], ns["k_dr"], ns["v_dr"]

    with ExitStack() as p1:
        e = p1.enter_context
        xl = e(nc.sbuf_tensor("xl", [128, CC, TA], BF16))
        cq_sb = e(nc.sbuf_tensor("cq_sb", [128, T], BF16))
        sq_sb = e(nc.sbuf_tensor("sq_sb", [128, T], BF16))
        ck_sb = e(nc.sbuf_tensor("ck_sb", [128, T], BF16))
        sk_sb = e(nc.sbuf_tensor("sk_sb", [128, T], BF16))
        xst = [e(nc.sbuf_tensor(f"xst{i}", [128, T], F32)) for i in range(2)]
        sqst = e(nc.sbuf_tensor("sqst", [128, T], BF16))
        a_bf = e(nc.sbuf_tensor("a_bf", [128, T], BF16))
        b_bf = e(nc.sbuf_tensor("b_bf", [128, T], BF16))
        wt = [e(nc.sbuf_tensor(f"wt{i}", [128, CC, 128], BF16)) for i in range(2)]
        qraw = e(nc.sbuf_tensor("qraw", [128, TH], BF16))
        t2 = e(nc.sbuf_tensor("t2", [128, TH], BF16))
        t3 = e(nc.sbuf_tensor("t3", [128, TH], BF16))
        qf = e(nc.sbuf_tensor("qf", [128, TH], BF16))
        vtst = e(nc.sbuf_tensor("vtst", [128, 8, 128], BF16))
        adst = e(nc.sbuf_tensor("adst", [128, 16], BF16))

        sdma(cq_sb[:], cosqD)
        sdma(sq_sb[:], sinqD)
        sdma(ck_sb[:], coskD)
        sdma(sk_sb[:], sinkD)

        gdma(xl[:, :, T:T + NA], adT.rearrange("(cc p) a -> p cc a", p=128))
        ad_gq = ctx.v("gq")
        ctx.wait(GP, "gq", ad_gq)
        GP.memset(xl[:, :, T + NA:TA], 0.0).then_inc(ctx.inc("gp"), 1)
        gp_pad = ctx.v("gp")

        xT_c = xT.rearrange("(cc p) t -> cc p t", p=128)
        copy_dv = {}
        for cc in range(CC):
            buf = xst[cc % 2]
            if cc >= 2:
                ctx.wait(SY, "dv", copy_dv[cc - 2])
            ld = sdma(buf[:], xT_c[cc])
            ctx.wait(VE, "sq", ld)
            VE.tensor_copy(xl[:, cc, 0:T], buf[:]).then_inc(ctx.inc("dv"), 1)
            copy_dv[cc] = ctx.v("dv")
            ctx.wait(SC, "dv", ctx.v("dv"))
            if cc == 0:
                ctx.wait(SC, "gp", ctx.v("gp"))
            if cc >= 1:
                ctx.wait(SC, "pe", ctx.v("pe"))
            SC.activation(sqst[:], xl[:, cc, 0:T], AF.Square).then_inc(ctx.inc("ac"), 1)
            ctx.wait(PE, "dv", ctx.v("dv"))
            ctx.wait(PE, "ac", ctx.v("ac"))
            for n in range(4):
                PE.matmul(pb[n][:], on_sb[:], xl[:, cc, n * 512:(n + 1) * 512],
                          start=(cc == 0), stop=(cc == CC - 1))
            last = None
            for n in range(4):
                last = PE.matmul(pb[4 + n][:], on_sb[:], sqst[:, n * 512:(n + 1) * 512],
                                 start=(cc == 0), stop=(cc == CC - 1))
            last.then_inc(ctx.inc("pe"), 1)
        pe_stats = ctx.v("pe")

        m_f, var_f = xst[0], xst[1]
        ctx.wait(VE, "pe", pe_stats)
        for n in range(4):
            sl = slice(n * 512, (n + 1) * 512)
            VE.tensor_scalar(m_f[:, sl], pb[n][:], 1.0 / C, None, op0=ALU.mult)
        for n in range(4):
            sl = slice(n * 512, (n + 1) * 512)
            VE.tensor_tensor(var_f[:, sl], m_f[:, sl], m_f[:, sl], op=ALU.mult)
        last = None
        for n in range(4):
            sl = slice(n * 512, (n + 1) * 512)
            last = VE.scalar_tensor_tensor(var_f[:, sl], pb[4 + n][:], 1.0 / C, var_f[:, sl],
                                           op0=ALU.mult, op1=ALU.subtract)
        last.then_inc(ctx.inc("dv"), 1)
        dv_var = ctx.v("dv")
        ctx.wait(SC, "dv", dv_var)
        SC.activation(var_f[:], var_f[:], AF.Ln, bias=LN_EPS, scale=1.0)
        SC.activation(a_bf[:], var_f[:], AF.Exp, scale=-0.5).then_inc(ctx.inc("ac"), 1)
        ctx.wait(VE, "ac", ctx.v("ac"))
        VE.scalar_tensor_tensor(b_bf[:], m_f[:], -1.0, a_bf[:],
                                op0=ALU.mult, op1=ALU.mult).then_inc(ctx.inc("dv"), 1)

        for cc in range(CC):
            for half in range(2):
                sl = slice(half * TH, (half + 1) * TH)
                tt = t2 if half == 0 else t3
                VE.tensor_tensor(tt[:], xl[:, cc, sl], a_bf[:, sl], op=ALU.mult)
                last = VE.tensor_tensor(xl[:, cc, sl], tt[:], b_bf[:, sl], op=ALU.add)
        last.then_inc(ctx.inc("dv"), 1)
        dv_ln1 = ctx.v("dv")
        if ns.get("dbg_xl") is not None:
            ctx.wait(SY, "dv", dv_ln1)
            ctx.wait(SY, "gp", ctx.v("gp"))
            ctx.wait(SY, "gq", ctx.v("gq"))
            sdma(ns["dbg_xl"], xl[:])

        pe_wt_free = [0, 0]
        dv_free_half = [ctx.v("dv"), ctx.v("dv")]
        dv_free4 = ctx.v("dv")
        gq_qf_free = [0, 0]
        gq_v_free = 0
        sq_vt_free = 0
        ctx.wait(PE, "gp", gp_pad)
        for oc in range(48):
            par = oc % 2
            wtb = wt[par]
            ctx.wait(GP, "pe", pe_wt_free[par])
            wld = gdma(wtb[:], wqkvT[:, oc * 128:(oc + 1) * 128]
                       .rearrange("(cc p) o -> p cc o", p=128))
            kind = "q" if oc < 16 else ("k" if oc < 32 else "v")
            head = oc % 16
            ctx.wait(PE, "gq", wld)
            for half in range(2):
                ctx.wait(PE, "dv", dv_free_half[half])
                last = None
                for cc in range(CC):
                    for n in range(2):
                        last = PE.matmul(pb[half * 2 + n][:], wtb[:, cc, :],
                                         xl[:, cc, half * TH + n * 512: half * TH + (n + 1) * 512],
                                         start=(cc == 0), stop=(cc == CC - 1))
                last.then_inc(ctx.inc("pe"), 1)

                if kind in ("q", "k"):
                    csb = cq_sb if kind == "q" else ck_sb
                    ssb = sq_sb if kind == "q" else sk_sb
                    dest = q_dr if kind == "q" else k_dr
                    ctx.wait(VE, "pe", ctx.v("pe"))
                    for n in range(2):
                        last = VE.tensor_copy(qraw[:, n * 512:(n + 1) * 512], pb[half * 2 + n][:])
                    last.then_inc(ctx.inc("dv"), 1)
                    dv_qraw = ctx.v("dv")
                    ctx.wait(PE, "dv", dv_qraw)
                    PE.matmul(pb[5][:], sw_sb[:], qraw[:, 0:512], start=True, stop=True)
                    PE.matmul(pb[6][:], sw_sb[:], qraw[:, 512:1024],
                              start=True, stop=True).then_inc(ctx.inc("pe"), 1)
                    pe_sw = ctx.v("pe")
                    hs = slice(half * TH, (half + 1) * TH)
                    ctx.wait(VE, "gq", gq_qf_free[half])
                    VE.tensor_tensor(t2[:], qraw[:], csb[:, hs], op=ALU.mult)
                    ctx.wait(VE, "pe", pe_sw)
                    VE.tensor_tensor(t3[:, 0:512], pb[5][:],
                                     ssb[:, half * TH: half * TH + 512], op=ALU.mult)
                    VE.tensor_tensor(t3[:, 512:1024], pb[6][:],
                                     ssb[:, half * TH + 512:(half + 1) * TH], op=ALU.mult)
                    VE.tensor_tensor(qf[:], t2[:], t3[:], op=ALU.add).then_inc(ctx.inc("dv"), 1)
                    dv_free_half[half] = ctx.v("dv")
                    ctx.wait(GP, "dv", ctx.v("dv"))
                    gq_qf_free[half] = gdma(dest[head][:, half * TH:(half + 1) * TH], qf[:])
                else:
                    ctx.wait(VE, "pe", ctx.v("pe"))
                    ctx.wait(VE, "sq", sq_vt_free)
                    for n in range(2):
                        last = VE.tensor_copy(qraw[:, n * 512:(n + 1) * 512], pb[half * 2 + n][:])
                    last.then_inc(ctx.inc("dv"), 1)
                    dv_free_half[half] = ctx.v("dv")
                    ctx.wait(SY, "dv", ctx.v("dv"))
                    ctx.wait(SY, "gq", gq_v_free)
                    h2 = ctx.inc("sq", 16)
                    SY.dma_start_transpose(vtst[:], qraw[:]).then_inc(h2, 16)
                    sq_vt_free = ctx.v("sq")
                    ctx.wait(SY, "sq", sq_vt_free)
                    ctx.wait(GP, "sq", sq_vt_free)
                    gq_v_free = gdma(
                        v_dr[head, half * 8:half * 8 + 8].rearrange("j p q -> p j q"),
                        vtst[:])
            if kind in ("k", "v"):
                ctx.wait(PE, "dv", dv_free4)
                last = None
                for cc in range(CC):
                    last = PE.matmul(pb[4][:, 0:16], wtb[:, cc, :], xl[:, cc, T:TA],
                                     start=(cc == 0), stop=(cc == CC - 1))
                last.then_inc(ctx.inc("pe"), 1)
                pe_ad = ctx.v("pe")
                ctx.wait(VE, "pe", pe_ad)
                if kind == "k":
                    VE.tensor_scalar(ak_sb[:, head, 0:16], pb[4][:, 0:16], RS_ADAPT, None,
                                     op0=ALU.mult).then_inc(ctx.inc("dv"), 1)
                    dv_free4 = ctx.v("dv")
                else:
                    if oc == 32:
                        VE.memset(adst[:], 0.0)
                    VE.tensor_copy(adst[:, 0:10], pb[4][:, 0:10]).then_inc(ctx.inc("dv"), 1)
                    dv_free4 = ctx.v("dv")
                    ctx.wait(PE, "dv", ctx.v("dv"))
                    PE.matmul(pb[7][0:16, 0:128], adst[:], id_sb[:],
                              start=True, stop=True).then_inc(ctx.inc("pe"), 1)
                    ctx.wait(VE, "pe", ctx.v("pe"))
                    VE.tensor_copy(av_sb[:, head, :], pb[7][0:16, 0:128]).then_inc(ctx.inc("dv"), 1)
            pe_wt_free[par] = ctx.v("pe")

        barrier(SY, GP, VE, SC, PE)



def emit_p2(nc, ctx, ns):
    SY, GP, VE, SC, PE = nc.sync, nc.gpsimd, nc.vector, nc.scalar, nc.tensor
    sdma, gdma, barrier = ns["sdma"], ns["gdma"], ns["barrier"]
    pb = ns["pb"]
    id_sb, mk_sb, gat_sb = ns["id_sb"], ns["mk_sb"], ns["gat_sb"]
    ak_sb, av_sb = ns["ak_sb"], ns["av_sb"]
    q_dr, k_dr, v_dr, yall = ns["q_dr"], ns["k_dr"], ns["v_dr"], ns["yall"]

    with ExitStack() as p2:
        e = p2.enter_context
        qh = [e(nc.sbuf_tensor(f"qh{i}", [128, T], BF16)) for i in range(2)]
        kh = [e(nc.sbuf_tensor(f"kh{i}", [128, T], BF16)) for i in range(2)]
        vh = [e(nc.sbuf_tensor(f"vh{i}", [128, NQT, 128], BF16)) for i in range(2)]
        ssb = e(nc.sbuf_tensor("ssb", [128, T], F32))
        asb = e(nc.sbuf_tensor("asb", [128, 16], F32))
        probs = e(nc.sbuf_tensor("probs", [128, T], BF16))
        probsN = e(nc.sbuf_tensor("probsN", [128, T], BF16))
        pT = [e(nc.sbuf_tensor(f"pT{i}", [128, NQT, 128], BF16)) for i in range(3)]
        paT = [e(nc.sbuf_tensor(f"paT{i}", [16, 128], BF16)) for i in range(3)]
        nm4 = e(nc.sbuf_tensor("nm4", [128, 4], F32))
        nmc = e(nc.sbuf_tensor("nmc", [128, 1], F32))
        sac4 = e(nc.sbuf_tensor("sac4", [128, 4], F32))
        ssumc = e(nc.sbuf_tensor("ssumc", [128, 1], F32))
        rinv = e(nc.sbuf_tensor("rinv", [128, 1], F32))
        anm = e(nc.sbuf_tensor("anm", [128, 1], F32))
        asac = e(nc.sbuf_tensor("asac", [128, 1], F32))
        arin = e(nc.sbuf_tensor("arin", [128, 1], F32))
        aprobs = e(nc.sbuf_tensor("aprobs", [128, 16], BF16))
        ystage = e(nc.sbuf_tensor("ystage", [128, 128], BF16))
        padt = e(nc.sbuf_tensor("padt", [128, 1], F32))
        mkb_sb = e(nc.sbuf_tensor("mkb_sb", [128, 128], BF16))

        VE.memset(aprobs[:], 0.0)
        VE.tensor_copy(mkb_sb[:], mk_sb[:]).then_inc(ctx.inc("dv"), 1)

        ld_done = [0, 0]
        pe_done_head = [0, 0]

        def load_head(head):
            par = head % 2
            ctx.wait(SY, "pe", pe_done_head[par])
            sdma(qh[par][:], q_dr[head])
            sdma(kh[par][:], k_dr[head])
            ld_done[par] = sdma(vh[par][:], v_dr[head].rearrange("k p q -> p k q"))

        load_head(0)
        load_head(1)
        st = {"dv_banks": ctx.v("dv"), "ac_ssb": 0, "dv_y": ctx.v("dv"),
              "gq_y": 0, "sq_pT": [0, 0, 0], "dv_paT": [ctx.v("dv")] * 3,
              "pe_pT_free": [0, 0, 0], "sq_pn": 0}

        def emit_scores(head, qt, par, bpar):
            n = (qt + 1) * 128
            nb = (n + 511) // 512
            qtile = qh[par][:, qt * 128:(qt + 1) * 128]
            ctx.wait(PE, "sq", ld_done[par])
            ctx.wait(PE, "dv", st["dv_banks"])
            jl, off = (qt * 128) // 512, (qt * 128) % 512
            for j in range(nb):
                nj = min(512, n - j * 512)
                PE.matmul(pb[j][:, 0:nj], qtile, kh[par][:, j * 512:j * 512 + nj],
                          start=True, stop=(j != jl))
                if j == jl:
                    PE.matmul(pb[jl][:, off:off + 128], id_sb[:], mkb_sb[:],
                              start=False, stop=True)
            PE.matmul(pb[4][:, 0:16], qtile, ak_sb[:, head, :],
                      start=True, stop=True).then_inc(ctx.inc("pe"), 1)
            pe_sc = ctx.v("pe")
            ctx.wait(VE, "pe", pe_sc)
            ctx.wait(VE, "ac", st["ac_ssb"])
            for j in range(nb):
                nj = min(512, n - j * 512)
                VE.tensor_copy(ssb[:, j * 512:j * 512 + nj], pb[j][:, 0:nj])
            VE.tensor_copy(asb[:], pb[4][:, 0:16]).then_inc(ctx.inc("dv"), 1)
            st["dv_banks"] = ctx.v("dv")
            for j in range(nb):
                nj = min(512, n - j * 512)
                VE.reduce_max(nm4[:, j:j + 1], ssb[:, j * 512:j * 512 + nj],
                              axis=AX.X, negate=True)
            VE.reduce_max(anm[:], asb[:, 0:10], axis=AX.X, negate=True)
            VE.tensor_copy(padt[:], nm4[:, 0:1])
            VE.tensor_copy(padt[:], nm4[:, 0:1])
            VE.tensor_reduce(nmc[:], nm4[:, 0:nb], axis=AX.X,
                             op=ALU.min).then_inc(ctx.inc("dv"), 1)
            dv_max = ctx.v("dv")
            ctx.wait(SC, "dv", dv_max)
            for j in range(nb):
                nj = min(512, n - j * 512)
                SC.activation(probs[:, j * 512:j * 512 + nj], ssb[:, j * 512:j * 512 + nj],
                              AF.Exp, bias=nmc[:], accum_out=sac4[:, j:j + 1])
            SC.activation(aprobs[:, 0:10], asb[:, 0:10], AF.Exp,
                          bias=anm[:], accum_out=asac[:]).then_inc(ctx.inc("ac"), 1)
            st["ac_ssb"] = ctx.v("ac")
            ctx.wait(VE, "ac", st["ac_ssb"])
            ctx.wait(VE, "sq", st["sq_pn"])
            if nb > 1:
                VE.tensor_reduce(ssumc[:], sac4[:, 0:nb], axis=AX.X, op=ALU.add)
                VE.tensor_copy(padt[:], nm4[:, 0:1])
                VE.tensor_copy(padt[:], nm4[:, 0:1])
                VE.reciprocal(rinv[:], ssumc[:])
            else:
                VE.reciprocal(rinv[:], sac4[:, 0:1])
            VE.reciprocal(arin[:], asac[:])
            VE.tensor_copy(padt[:], nm4[:, 0:1])
            VE.tensor_copy(padt[:], nm4[:, 0:1])
            VE.tensor_scalar(probsN[:, 0:n], probs[:, 0:n], rinv[:], None, op0=ALU.mult)
            VE.tensor_scalar(aprobs[:, 0:10], aprobs[:, 0:10], arin[:], gat_sb[:],
                             op0=ALU.mult, op1=ALU.mult).then_inc(ctx.inc("dv"), 1)
            dv_pn = ctx.v("dv")
            ctx.wait(SY, "dv", dv_pn)
            ctx.wait(SY, "pe", st["pe_pT_free"][bpar])
            h2 = ctx.inc("sq", 16)
            SY.dma_start_transpose(pT[bpar][:, 0:qt + 1, :], probsN[:, 0:n]).then_inc(h2, 16)
            st["sq_pT"][bpar] = st["sq_pn"] = ctx.v("sq")
            ctx.wait(SY, "sq", ctx.v("sq"))
            ctx.wait(PE, "dv", dv_pn)
            ctx.wait(PE, "dv", st["dv_paT"][bpar])
            PE.matmul(pb[7][0:16, 0:128], aprobs[:], id_sb[:],
                      start=True, stop=True).then_inc(ctx.inc("pe"), 1)
            ctx.wait(VE, "pe", ctx.v("pe"))
            VE.tensor_copy(paT[bpar][:], pb[7][0:16, 0:128]).then_inc(ctx.inc("dv"), 1)
            st["dv_paT"][bpar] = ctx.v("dv")

        def emit_av(head, qt, par, bpar):
            ctx.wait(PE, "sq", st["sq_pT"][bpar])
            ctx.wait(PE, "dv", max(st["dv_y"], st["dv_paT"][bpar]))
            for kt in range(qt + 1):
                PE.matmul(pb[6][:, 0:128], vh[par][:, kt, :], pT[bpar][:, kt, :],
                          start=(kt == 0), stop=False)
            PE.matmul(pb[6][:, 0:128], av_sb[:, head, :], paT[bpar][:],
                      start=False, stop=True).then_inc(ctx.inc("pe"), 1)
            st["pe_pT_free"][bpar] = ctx.v("pe")
            ctx.wait(VE, "pe", ctx.v("pe"))
            ctx.wait(VE, "gq", st["gq_y"])
            VE.tensor_copy(ystage[:], pb[6][:, 0:128]).then_inc(ctx.inc("dv"), 1)
            st["dv_y"] = ctx.v("dv")
            ctx.wait(GP, "dv", ctx.v("dv"))
            st["gq_y"] = gdma(yall[head][:, qt * 128:(qt + 1) * 128], ystage[:])

        pairs = [(h_, q_) for h_ in range(NH) for q_ in range(NQT)]
        from collections import deque
        pend = deque()
        for idx, (head, qt) in enumerate(pairs):
            if qt == 2 and head + 1 < NH:
                load_head(head + 1)
            emit_scores(head, qt, head % 2, idx % 3)
            if len(pend) >= 2:
                ph, pq, pi = pend.popleft()
                emit_av(ph, pq, ph % 2, pi % 3)
                if pq == NQT - 1:
                    pe_done_head[ph % 2] = ctx.v("pe")
            pend.append((head, qt, idx))
        while pend:
            ph, pq, pi = pend.popleft()
            emit_av(ph, pq, ph % 2, pi % 3)
            if pq == NQT - 1:
                pe_done_head[ph % 2] = ctx.v("pe")

        barrier(SY, GP, VE, SC, PE)


def emit_exchange(nc, ctx, ns):
    SY, GP, VE, SC, PE = nc.sync, nc.gpsimd, nc.vector, nc.scalar, nc.tensor
    yall, ymine, ypart = ns["yall"], ns["ymine"], ns["ypart"]
    sh_y, sh_fl = ns["sh_y"], ns["sh_fl"]
    non_sb, fl_sb = ns["non_sb"], ns["fl_sb"]

    import copy
    save = dict(ctx.cnt)
    post = None
    for arm in nc.switch_core_id(engines=[nc.gpsimd], n=128):
        ctx.cnt = dict(save)
        a = int(arm)
        slot = PNC.index(a % 8)
        h = slot % 2
        GP.dma_start(out=ymine, in_=yall[:, :, h * TH:(h + 1) * TH]).then_inc(ctx.inc("gq", 16), 16)
        GP.dma_start(out=sh_y[slot], in_=yall[:, :, (1 - h) * TH:(2 - h) * TH]).then_inc(ctx.inc("gq", 16), 16)
        ctx.wait(GP, "gq")
        GP.dma_start(out=sh_fl[slot, 0], in_=non_sb[:]).then_inc(ctx.inc("gq", 16), 16)
        ctx.wait(GP, "gq")
        with (
            nc.gpsimd.register(f"xc{a}") as cont,
            nc.gpsimd.register(f"xf{a}") as fv,
            nc.gpsimd.register(f"xn{a}") as nv,
            nc.gpsimd.register(f"xe{a}") as exp,
        ):
            GP.reg_load(nv, non_sb[0:1, 0:1])
            GP.reg_mov(cont, 1)
            GP.reg_mov(exp, 16)
            with nc.gpsimd.While(cont):
                GP.dma_start(out=fl_sb[:], in_=sh_fl[slot ^ 1, 0]).then_inc(ctx.sem("px"), 16)
                GP.wait_ge(ctx.sem("px"), exp)
                GP.reg_alu(exp, exp, 16, ALU.add)
                GP.reg_load(fv, fl_sb[0:1, 0:1])
                GP.reg_alu(cont, fv, nv, ALU.not_equal)
        GP.dma_start(out=ypart, in_=sh_y[slot ^ 1]).then_inc(ctx.inc("gq", 16), 16)
        ctx.wait(GP, "gq")
        post = dict(ctx.cnt)
    ctx.cnt = post
    ns["barrier"](SY, GP, VE, SC, PE)




def emit_p34(nc, ctx, ns):
    SY, GP, VE, SC, PE = nc.sync, nc.gpsimd, nc.vector, nc.scalar, nc.tensor
    sdma, gdma, barrier = ns["sdma"], ns["gdma"], ns["barrier"]
    pb = ns["pb"]
    id_sb, on_sb = ns["id_sb"], ns["on_sb"]
    ymine, ypart, x1_dr = ns["ymine"], ns["ypart"], ns["x1_dr"]
    wprojT, xTh, yout = ns["wprojT"], ns["xTh"], ns["yout"]
    gateT, upT, dnT = ns["gateT"], ns["upT"], ns["dnT"]
    gsT, usT, dsT = ns["gsT"], ns["usT"], ns["dsT"]
    sspill, wdeq_dr = ns["sspill"], ns["wdeq_dr"]

    with ExitStack() as p34:
        e = p34.enter_context
        x2_st = e(ExitStack())
        x2buf = x2_st.enter_context(nc.sbuf_tensor("x2buf", [128, CC, TH], BF16, side="right"))

        with ExitStack() as p3:
            e3 = p3.enter_context
            ybuf = e3(nc.sbuf_tensor("ybuf", [128, CC, TH], BF16))
            wt = [e3(nc.sbuf_tensor(f"pwt{i}", [128, CC, 128], BF16)) for i in range(2)]
            xst = [e3(nc.sbuf_tensor(f"pxst{i}", [128, TH], F32)) for i in range(2)]
            x1f = e3(nc.sbuf_tensor("x1f", [128, TH], F32))
            sqb = e3(nc.sbuf_tensor("sqb", [128, TH], BF16))
            a2 = e3(nc.sbuf_tensor("a2", [128, TH], BF16))
            b2 = e3(nc.sbuf_tensor("b2", [128, TH], BF16))
            t2b = e3(nc.sbuf_tensor("t2b", [128, TH], BF16))

            gdma(ybuf[:, 0:NH, :], ymine.rearrange("j p t -> p j t"))
            yld = gdma(ybuf[:, NH:CC, :], ypart.rearrange("j p t -> p j t"))

            pe_wt_free = [0, 0]
            dv_banks_free = ctx.v("dv")
            gq_x1f_free = 0
            for oc in range(CC):
                par = oc % 2
                wtb = wt[par]
                ctx.wait(GP, "pe", pe_wt_free[par])
                wld = gdma(wtb[:], wprojT[:, oc * 128:(oc + 1) * 128]
                           .rearrange("(cc p) o -> p cc o", p=128))
                ctx.wait(SY, "dv", dv_banks_free)
                xld = sdma(xst[par][:], xTh.rearrange("(cc p) t -> cc p t", p=128)[oc])
                ctx.wait(PE, "gq", max(wld, yld))
                ctx.wait(PE, "dv", dv_banks_free)
                bs = 2 * par
                last = None
                for cc in range(CC):
                    for n in range(2):
                        last = PE.matmul(pb[bs + n][:], wtb[:, cc, :],
                                         ybuf[:, cc, n * 512:(n + 1) * 512],
                                         start=(cc == 0), stop=(cc == CC - 1))
                last.then_inc(ctx.inc("pe"), 1)
                pe_wt_free[par] = ctx.v("pe")
                ctx.wait(VE, "pe", ctx.v("pe"))
                ctx.wait(VE, "sq", xld)
                ctx.wait(VE, "gq", gq_x1f_free)
                for n in range(2):
                    VE.tensor_tensor(x1f[:, n * 512:(n + 1) * 512], pb[bs + n][:],
                                     xst[par][:, n * 512:(n + 1) * 512], op=ALU.add)
                VE.tensor_copy(x2buf[:, oc, :], x1f[:]).then_inc(ctx.inc("dv"), 1)
                dv_banks_free = ctx.v("dv")
                ctx.wait(GP, "dv", ctx.v("dv"))
                gq_x1f_free = gdma(x1_dr[oc], x1f[:])

            pe_sq_free = 0
            for cc in range(CC):
                ctx.wait(SC, "dv", dv_banks_free if cc == 0 else ctx.v("dv"))
                ctx.wait(SC, "pe", pe_sq_free)
                SC.activation(sqb[:], x2buf[:, cc, :], AF.Square).then_inc(ctx.inc("ac"), 1)
                ctx.wait(PE, "ac", ctx.v("ac"))
                if cc == 0:
                    ctx.wait(PE, "dv", dv_banks_free)
                last = None
                for n in range(2):
                    PE.matmul(pb[4 + n][:], on_sb[:], x2buf[:, cc, n * 512:(n + 1) * 512],
                              start=(cc == 0), stop=(cc == CC - 1))
                    last = PE.matmul(pb[6 + n][:], on_sb[:], sqb[:, n * 512:(n + 1) * 512],
                                     start=(cc == 0), stop=(cc == CC - 1))
                last.then_inc(ctx.inc("pe"), 1)
                pe_sq_free = ctx.v("pe")
            pe_stats2 = ctx.v("pe")
            m2f, v2f = xst[0], xst[1]
            ctx.wait(VE, "pe", pe_stats2)
            for n in range(2):
                sl = slice(n * 512, (n + 1) * 512)
                VE.tensor_scalar(m2f[:, sl], pb[4 + n][:], 1.0 / C, None, op0=ALU.mult)
            for n in range(2):
                sl = slice(n * 512, (n + 1) * 512)
                VE.tensor_tensor(v2f[:, sl], m2f[:, sl], m2f[:, sl], op=ALU.mult)
            last = None
            for n in range(2):
                sl = slice(n * 512, (n + 1) * 512)
                last = VE.scalar_tensor_tensor(v2f[:, sl], pb[6 + n][:], 1.0 / C, v2f[:, sl],
                                               op0=ALU.mult, op1=ALU.subtract)
            last.then_inc(ctx.inc("dv"), 1)
            ctx.wait(SC, "dv", ctx.v("dv"))
            SC.activation(v2f[:], v2f[:], AF.Ln, bias=LN_EPS, scale=1.0)
            SC.activation(a2[:], v2f[:], AF.Exp, scale=-0.5).then_inc(ctx.inc("ac"), 1)
            ctx.wait(VE, "ac", ctx.v("ac"))
            VE.scalar_tensor_tensor(b2[:], m2f[:], -1.0, a2[:],
                                    op0=ALU.mult, op1=ALU.mult).then_inc(ctx.inc("dv"), 1)
            for cc in range(CC):
                VE.tensor_tensor(t2b[:], x2buf[:, cc, :], a2[:], op=ALU.mult)
                last = VE.tensor_tensor(x2buf[:, cc, :], t2b[:], b2[:], op=ALU.add)
            last.then_inc(ctx.inc("dv"), 1)
            barrier(SY, GP, VE, SC, PE)

        s_st = p34.enter_context(ExitStack())
        s_sbuf = s_st.enter_context(nc.sbuf_tensor("s_sbuf", [128, IC, 512], BF16))

        with ExitStack() as pa:
            ea = pa.enter_context
            gsl = [[ea(nc.sbuf_tensor(f"gsl{m}{i}", [128, 32], F32)) for i in range(2)]
                   for m in range(2)]
            wg_sb = [ea(nc.sbuf_tensor(f"wg_sb{i}", [128, CC, 128], BF16)) for i in range(2)]
            wu_sb = [ea(nc.sbuf_tensor(f"wu_sb{i}", [128, CC, 128], BF16)) for i in range(2)]
            qstage = ea(nc.sbuf_tensor("qstage", [128, T], I32))
            wdq = ea(nc.sbuf_tensor("wdq", [128, T], BF16))
            tbst = ea(nc.sbuf_tensor("tbst", [128, 512], BF16))
            tbst2 = ea(nc.sbuf_tensor("tbst2", [128, 512], BF16))
            s2st = ea(nc.sbuf_tensor("s2st", [128, 512], BF16))

            dv_wdq_free = ctx.v("dv")
            sq_wdq_free = 0
            sc_gu_used = ctx.v("ac")
            dv_gu_used = ctx.v("dv")
            gq_s2_free = 0
            pe_w_used = [0, 0]
            sq_w_ready = [0, 0]

            def emit_deq_a(ic):
                nonlocal dv_wdq_free, sq_wdq_free
                buf = ic % 2
                for mi, (wqd, sclT, wsbs) in enumerate(
                        [(gateT, gsT, wg_sb), (upT, usT, wu_sb)]):
                    ctx.wait(GP, "dv", dv_wdq_free)
                    scl = gsl[mi][ic % 2]
                    scl_ld = gdma(scl[:], sclT[ic * 128:(ic + 1) * 128, :])
                    for hh in range(2):
                        ctx.wait(GP, "sq", sq_wdq_free)
                        ctx.wait(GP, "dv", dv_wdq_free)
                        qld = gdma(qstage[:], wqd[ic * 128:(ic + 1) * 128,
                                                  hh * 2048:(hh + 1) * 2048])
                        ctx.wait(VE, "gq", max(qld, scl_ld))
                        ctx.wait(VE, "sq", sq_wdq_free)
                        sv = scl[:, hh * 16:(hh + 1) * 16].unsqueeze(-1) \
                            .broadcast_to([128, 16, 128])
                        VE.scalar_tensor_tensor(
                            wdq[:].rearrange("p (g r) -> p g r", r=128),
                            qstage[:].rearrange("p (g r) -> p g r", r=128),
                            8.0, sv, op0=ALU.subtract, op1=ALU.mult,
                        ).then_inc(ctx.inc("dv"), 1)
                        dv_wdq_free = ctx.v("dv")
                        ctx.wait(SY, "dv", dv_wdq_free)
                        ctx.wait(SY, "pe", pe_w_used[buf])
                        h2 = ctx.inc("sq", 16)
                        SY.dma_start_transpose(wsbs[buf][:, hh * 16:(hh + 1) * 16, :],
                                               wdq[:]).then_inc(h2, 16)
                        sq_wdq_free = ctx.v("sq")
                        ctx.wait(SY, "sq", sq_wdq_free)
                sq_w_ready[buf] = ctx.v("sq")

            emit_deq_a(0)
            for ic in range(IC):
                buf = ic % 2
                if ic + 1 < IC:
                    emit_deq_a(ic + 1)
                ctx.wait(PE, "sq", sq_w_ready[buf])
                ctx.wait(PE, "ac", sc_gu_used)
                ctx.wait(PE, "dv", dv_gu_used)
                last = None
                for hc in range(CC):
                    for n in range(2):
                        PE.matmul(pb[n][:], wg_sb[buf][:, hc, :],
                                  x2buf[:, hc, n * 512:(n + 1) * 512],
                                  start=(hc == 0), stop=(hc == CC - 1))
                        last = PE.matmul(pb[2 + n][:], wu_sb[buf][:, hc, :],
                                         x2buf[:, hc, n * 512:(n + 1) * 512],
                                         start=(hc == 0), stop=(hc == CC - 1))
                last.then_inc(ctx.inc("pe"), 1)
                pe_w_used[buf] = pe_gu = ctx.v("pe")
                ctx.wait(SC, "pe", pe_gu)
                ctx.wait(SC, "dv", dv_gu_used)
                SC.activation(tbst[:], pb[0][:], AF.Silu).then_inc(ctx.inc("ac"), 1)
                ctx.wait(VE, "ac", ctx.v("ac"))
                VE.tensor_tensor(s_sbuf[:, ic, :], tbst[:], pb[2][:], op=ALU.mult) \
                    .then_inc(ctx.inc("dv"), 1)
                SC.activation(tbst2[:], pb[1][:], AF.Silu).then_inc(ctx.inc("ac"), 1)
                sc_gu_used = ctx.v("ac")
                ctx.wait(VE, "ac", ctx.v("ac"))
                ctx.wait(VE, "gq", gq_s2_free)
                VE.tensor_tensor(s2st[:], tbst2[:], pb[3][:], op=ALU.mult) \
                    .then_inc(ctx.inc("dv"), 1)
                dv_gu_used = ctx.v("dv")
                ctx.wait(GP, "dv", ctx.v("dv"))
                gq_s2_free = gdma(sspill[ic], s2st[:])
            barrier(SY, GP, VE, SC, PE)
        x2_st.close()



        with ExitStack() as pbx:
            eb = pbx.enter_context
            dsl = [eb(nc.sbuf_tensor(f"dsl{i}", [128, 86], F32)) for i in range(2)]
            dstage = [eb(nc.sbuf_tensor(f"dstage{i}", [128, 5504], I32)) for i in range(2)]
            ddq = [eb(nc.sbuf_tensor(f"ddq{i}", [128, 5504], BF16)) for i in range(2)]
            wdt = [eb(nc.sbuf_tensor(f"wdt{i}", [128, 43, 128], BF16)) for i in range(2)]
            x1st = eb(nc.sbuf_tensor("x1st", [128, 512], F32))
            outst = eb(nc.sbuf_tensor("outst", [128, 512], F32))

            dv_dd_free = [ctx.v("dv")] * 2
            sq_dd_free = [0, 0]
            dv_ds_free = [ctx.v("dv")] * 2
            pe_wdt_free = [0, 0]
            gq_wdt = [0, 0]
            for thalf in range(2):
                if thalf == 1:
                    barrier(SY, GP, VE, SC, PE)
                    sld = sdma(s_sbuf[:], sspill.rearrange("i p n -> p i n"))
                for hc in range(CC):
                    pbD = pb[6 + hc % 2]
                    if thalf == 0:
                        dsld_g = gdma(dsl[hc % 2][:], dsT[hc * 128:(hc + 1) * 128, :])
                        dsld = 0
                        ctx.wait(VE, "gq", dsld_g)
                    for ih in range(2):
                        buf = (hc * 2 + ih) % 2
                        if thalf == 0:
                            ctx.wait(GP, "dv", dv_ds_free[buf])
                            dld = gdma(dstage[buf][:], dnT[hc * 128:(hc + 1) * 128,
                                                           ih * 5504:(ih + 1) * 5504])
                            ctx.wait(VE, "gq", dld)
                            ctx.wait(VE, "sq", max(dsld, sq_dd_free[buf]))
                            dv_ = dsl[hc % 2][:, ih * 43:(ih + 1) * 43].unsqueeze(-1) \
                                .broadcast_to([128, 43, 128])
                            VE.scalar_tensor_tensor(
                                ddq[buf][:].rearrange("p (g r) -> p g r", r=128),
                                dstage[buf][:].rearrange("p (g r) -> p g r", r=128),
                                8.0, dv_, op0=ALU.subtract, op1=ALU.mult,
                            ).then_inc(ctx.inc("dv"), 1)
                            dv_ds_free[buf] = ctx.v("dv")
                            ctx.wait(SY, "dv", ctx.v("dv"))
                            ctx.wait(SY, "pe", pe_wdt_free[buf])
                            ctx.wait(SY, "gq", gq_wdt[buf])
                            h2 = ctx.inc("sq", 16)
                            SY.dma_start_transpose(wdt[buf][:], ddq[buf][:]).then_inc(h2, 16)
                            sq_dd_free[buf] = ctx.v("sq")
                            ctx.wait(SY, "sq", sq_dd_free[buf])
                            ctx.wait(GP, "sq", ctx.v("sq"))
                            gq_wdt[buf] = gdma(wdeq_dr[hc, ih * 43:(ih + 1) * 43]
                                               .rearrange("i p q -> p i q"), wdt[buf][:])
                            ctx.wait(PE, "sq", sq_dd_free[buf])
                        else:
                            ctx.wait(GP, "pe", pe_wdt_free[buf])
                            wld = gdma(wdt[buf][:], wdeq_dr[hc, ih * 43:(ih + 1) * 43]
                                       .rearrange("i p q -> p i q"))
                            ctx.wait(PE, "gq", wld)
                            if hc == 0 and ih == 0:
                                ctx.wait(PE, "sq", sld)
                        if ih == 0:
                            ctx.wait(PE, "dv", ctx.v("dv"))
                        last = None
                        for i43 in range(43):
                            last = PE.matmul(pbD[:], wdt[buf][:, i43, :],
                                             s_sbuf[:, ih * 43 + i43, :],
                                             start=(ih == 0 and i43 == 0),
                                             stop=(ih == 1 and i43 == 42))
                        last.then_inc(ctx.inc("pe"), 1)
                        pe_wdt_free[buf] = ctx.v("pe")
                    pe_dn = ctx.v("pe")
                    xld = sdma(x1st[:], x1_dr[hc][:, thalf * 512:(thalf + 1) * 512])
                    ctx.wait(VE, "pe", pe_dn)
                    ctx.wait(VE, "sq", xld)
                    ctx.wait(VE, "so", ctx.v("so"))
                    VE.tensor_tensor(outst[:], pbD[:], x1st[:], op=ALU.add) \
                        .then_inc(ctx.inc("dv"), 1)
                    ctx.wait(SY, "dv", ctx.v("dv"))
                    h2 = ctx.inc("so", 16)
                    SY.dma_start(out=yout[hc * 128:(hc + 1) * 128,
                                          thalf * 512:(thalf + 1) * 512],
                                 in_=outst[:]).then_inc(h2, 16)
            barrier(SY, GP, VE, SC, PE)




def rope_tables():
    theta = 1.0 / (10000.0 ** (np.arange(0, HD, 2, dtype=np.float64) / HD))
    idx = np.outer(np.arange(T, dtype=np.float64), theta)
    cos = np.cos(idx)
    sin = np.sin(idx)
    Cf = np.concatenate([cos, cos], axis=1).T
    Sf = np.concatenate([-sin, sin], axis=1).T
    rs = 1.0 / float(np.sqrt(HD))
    return Cf, Sf, rs


def prep_core_inputs(inp, core, nonce):
    import ml_dtypes
    bf = ml_dtypes.bfloat16
    b, h = core // 2, core % 2
    x = np.asarray(inp["L0_attn_x"])
    w_attn = np.asarray(inp["L0_attn_w_attn"])
    w_proj = np.asarray(inp["L0_attn_w_proj"])
    ad = np.asarray(inp["L0_attn_adapter_wte"])
    gating = np.asarray(inp["L0_attn_gating"])
    gq = np.asarray(inp["L0_ffn_gate_q"])
    gs = np.asarray(inp["L0_ffn_gate_s"])
    uq = np.asarray(inp["L0_ffn_up_q"])
    us = np.asarray(inp["L0_ffn_up_s"])
    dq = np.asarray(inp["L0_ffn_down_q"])
    ds = np.asarray(inp["L0_ffn_down_s"])

    xT_ = np.ascontiguousarray(x[b].T.astype(np.float32))
    xTh_ = np.ascontiguousarray(xT_[:, h * TH:(h + 1) * TH])
    rows = np.r_[h * 2048:(h + 1) * 2048,
                 4096 + h * 2048: 4096 + (h + 1) * 2048,
                 8192 + h * 2048: 8192 + (h + 1) * 2048]
    wqkvT_ = np.ascontiguousarray(w_attn[rows].T)
    perm = np.r_[h * 2048:(h + 1) * 2048, (1 - h) * 2048:(2 - h) * 2048]
    wprojT_ = np.ascontiguousarray(w_proj.T[perm])
    Cf, Sf, rs = rope_tables()
    mask = np.where(np.arange(128)[None, :] <= np.arange(128)[:, None], 0.0, -1e9)

    return dict(
        xT=xT_, xTh=xTh_, wqkvT=wqkvT_, wprojT=wprojT_,
        adT=np.ascontiguousarray(ad.T),
        gating128=np.full((128, 1), np.float32(gating[0])),
        gateT=np.ascontiguousarray(gq.T), upT=np.ascontiguousarray(uq.T),
        dnT=np.ascontiguousarray(dq.T),
        gsT=np.ascontiguousarray(gs.T), usT=np.ascontiguousarray(us.T),
        dsT=np.ascontiguousarray(ds.T),
        cosqD=(Cf * rs).astype(bf), sinqD=(Sf * rs).astype(bf),
        coskD=Cf.astype(bf), sinkD=Sf.astype(bf),
        identD=np.eye(128).astype(bf),
        swapD=np.eye(128)[(np.arange(128) + 64) % 128].astype(bf),
        onesD=np.ones((128, 128)).astype(bf),
        maskD=mask.astype(np.float32),
        nonceD=nonce,
    )


from concourse.bass_utils import run_bass_kernel_spmd

_NC_CACHE = [None]


def kernel(**inputs):
    if _NC_CACHE[0] is None:
        _NC_CACHE[0] = mk_build(debug=False)
    nc = _NC_CACHE[0]
    nonce = np.random.randint(1, 2 ** 30, size=(1, 128)).astype(np.int32)
    in_maps = [prep_core_inputs(inputs, c, nonce) for c in range(8)]
    res = run_bass_kernel_spmd(nc, in_maps, core_ids=list(range(8)))
    out = np.empty((4, 2048, 4096), np.float32)
    for c in range(8):
        b, h = c // 2, c % 2
        out[b, h * TH:(h + 1) * TH, :] = np.asarray(res.results[c]["yout"]).T
    return out
